# version 06
# baseline (speedup 1.0000x reference)
import contextlib
import numpy as np
import concourse.bass as bass
import concourse.mybir as mybir

F32 = mybir.dt.float32
BF16 = mybir.dt.bfloat16
I32 = mybir.dt.int32
ALU = mybir.AluOpType
AF = mybir.ActivationFunctionType
AX = mybir.AxisListType.X

D = 1024
K8 = D // 128
I_FFN = 3072
IC = I_FFN // 128
TW = 512
LN_EPS = 1e-5
DEN_EPS = 1e-6
R32767 = float(np.float32(1.0) / np.float32(32767.0))
R1024 = 1.0 / 1024.0
C65535 = 65535.0 / 65536.0
R65536 = 1.0 / 65536.0
R7 = float(np.float32(1.0) / np.float32(7.0))


class Sem:
    def __init__(self, stack, nc, name):
        self.h = stack.enter_context(nc.semaphore(name))
        self.n = 0

    def inc(self, instr, k=1):
        instr.then_inc(self.h, k)
        self.n += k
        return self.n


def build(TOK, dbg=False, stop_after=None, p1a_cut=99):
    G = TOK // TW
    nc = bass.Bass("TRN2", target_bir_lowering=False, detect_race_conditions=False)
    st_ = contextlib.ExitStack()

    def din(name, shape, dt=F32):
        return nc.dram_tensor(name, shape, dt, kind="ExternalInput").ap()

    xfm = din("xfm", [D, TOK])
    wqkv = din("wqkv", [D, 3 * D])
    wgate_a = din("wgate_a", [D, D])
    wproj = din("wproj", [D, D])
    bqk = din("bqk", [128, 24])
    bv_row = din("bv_row", [1, D])
    bproj_d = din("bproj_d", [128, 8])
    lng_d = din("lng_d", [128, 8])
    lnb_d = din("lnb_d", [128, 8])
    wup_d = din("wup_d", [I_FFN, D])
    wgf_d = din("wgf_d", [I_FFN, D])
    wdn_d = din("wdn_d", [D, I_FFN])
    identb_d = din("identb_d", [128, 128], BF16)
    identf_d = din("identf_d", [128, 128])
    onesrow_d = din("onesrow_d", [1, 128])
    onescol_d = din("onescol_d", [128, 1], BF16)
    mask_d = din("mask_d", [128, TW])
    flagr_d = din("flagr_d", [128, 1])
    flags_d = din("flags_d", [128, 1])

    outfm = nc.dram_tensor("outfm", [D, TOK], F32, kind="ExternalOutput").ap()
    dbg_out = {}
    if dbg:
        for nm, shape, dt in [
            ("d_xn", [K8, 128, TW], F32), ("d_xn2", [K8, 128, TW], BF16),
            ("d_kfm", [K8, 128, TW], BF16), ("d_vtm", [4, 128, D], BF16),
            ("d_S", [128, K8 * D], F32), ("d_ksum", [128, 8], F32),
            ("d_scl", [128, 16], F32), ("d_gt0", [128, TOK], F32),
            ("d_up0", [128, TOK], F32), ("d_g20", [128, 1024], F32),
            ("d_o0", [128, 1024], F32), ("d_part", [K8, 128, TW], F32),
            ("d_q", [K8, 128, TW], BF16), ("d_y", [K8, 128, TW], BF16),
            ("d_den", [1, TW], F32), ("d_wup0", [K8, 128, 128], BF16),
            ("d_xq", [K8, 128, 1024], BF16), ("d_W1", [128, 8192], BF16),
        ]:
            dbg_out[nm] = nc.dram_tensor(nm, shape, dt, kind="ExternalOutput").ap()

    s_xn = nc.dram_tensor("s_xn", [K8, 128, TOK], F32).ap()
    s_xn2 = nc.dram_tensor("s_xn2", [K8, 128, TOK], BF16).ap()
    s_qfm = nc.dram_tensor("s_qfm", [K8, 128, TOK], BF16).ap()
    s_kfm = nc.dram_tensor("s_kfm", [K8, 128, TOK], BF16).ap()
    s_ktm = nc.dram_tensor("s_ktm", [TOK // 128, 128, D], BF16).ap()
    s_vtm = nc.dram_tensor("s_vtm", [TOK // 128, 128, D], BF16).ap()
    s_up = nc.dram_tensor("s_up", [IC, 128, TOK], F32).ap()
    s_gt = nc.dram_tensor("s_gt", [IC, 128, TOK], F32).ap()
    s_g2 = nc.dram_tensor("s_g2", [IC, 128, TOK], F32).ap()
    s_o = nc.dram_tensor("s_o", [IC, 128, TOK], F32).ap()
    s_part = nc.dram_tensor("s_part", [K8, 128, TOK], F32).ap()
    s_wupT = nc.dram_tensor("s_wupT", [IC, K8, 128, 128], BF16).ap()
    s_wgtT = nc.dram_tensor("s_wgtT", [IC, K8, 128, 128], BF16).ap()
    s_wdnT = nc.dram_tensor("s_wdnT", [K8, IC, 128, 128], BF16).ap()
    cc_i, cc_o = {}, {}
    for nm in ["m1", "m23", "m4", "m5"]:
        cc_i[nm] = nc.dram_tensor(f"cc_{nm}_i", [128, 1], F32).ap()
        cc_o[nm] = nc.dram_tensor(f"cc_{nm}_o", [128, 1], F32).ap()
    cc_si = nc.dram_tensor("cc_si", [128, K8 * D + 8], F32).ap()
    cc_so = nc.dram_tensor("cc_so", [128, K8 * D + 8], F32).ap()

    def sb(name, shape, dt):
        return nc.alloc_sbuf_tensor(name, shape, dt).ap()

    identb = sb("identb", [128, 128], BF16)
    identf = sb("identf", [128, 128], F32)
    onesrow = sb("onesrow", [1, 128], F32)
    onescol = sb("onescol", [128, 1], BF16)
    maskt = sb("maskt", [128, TW], F32)
    flagr = sb("flagr", [128, 1], F32)
    flags = sb("flags", [128, 1], F32)
    bqk_s = sb("bqk_s", [128, 24], F32)
    bproj_s = sb("bproj_s", [128, 8], F32)
    lng_s = sb("lng_s", [128, 8], F32)
    lnb_s = sb("lnb_s", [128, 8], F32)
    bvtile = sb("bvtile", [128, D], F32)
    sclbc = sb("sclbc", [128, 16], F32)
    rows = sb("rows", [1, 3616], F32)
    ksum = sb("ksum", [128, 8], F32)
    ksum_b = sb("ksum_b", [128, 8], BF16)
    m1acc = sb("m1acc", [128, 1], F32)
    m2acc = sb("m2acc", [128, 1], F32)
    m3acc = sb("m3acc", [128, 1], F32)
    m4acc = sb("m4acc", [128, 1], F32)
    m5acc = sb("m5acc", [128, 1], F32)
    dmy0 = sb("dmy0", [128, 64], F32)
    dmy1 = sb("dmy1", [128, 64], F32)
    mg128 = sb("mg128", [128, 1], F32)
    mr128 = sb("mr128", [128, 1], F32)
    S = sb("S", [128, K8 * D], F32)
    Sb = sb("Sb", [128, K8 * D], BF16)

    def ps(name, shape, dt=F32):
        return nc.alloc_psum_tensor(name, shape, dt).ap()

    psA = ps("psA", [128, TW])
    psB = ps("psB", [128, TW])
    psC = ps("psC", [128, TW])
    psD = ps("psD", [128, TW])
    psS1 = ps("psS1", [128, 512])
    psS2 = ps("psS2", [128, 512])
    psT = ps("psT", [128, 2048], BF16)
    psRot = [psA, psB, psC, psD]

    LD = Sem(st_, nc, "LD")
    LDG = Sem(st_, nc, "LDG")
    SO = Sem(st_, nc, "SO")
    PEc = Sem(st_, nc, "PEc")
    DVc = Sem(st_, nc, "DVc")
    ACc = Sem(st_, nc, "ACc")
    GPc = Sem(st_, nc, "GPc")
    CCc = Sem(st_, nc, "CCc")
    BARc = Sem(st_, nc, "BARc")

    block = st_.enter_context(nc.Block())

    def VSP(v, n=2):
        for _ in range(n):
            v.tensor_copy(dmy1[:, :], dmy0[:, :])

    def _finish():
        fin0 = SO.n

        def fs(sync):
            sync.wait_ge(SO.h, fin0)
            sync.wait_ge(LD.h, LD.n)
        block.sync(fs)
        st_.close()
    bar_total = [0]

    def barrier():
        waits = [(PEc.h, PEc.n), (DVc.h, DVc.n), (ACc.h, ACc.n),
                 (LD.h, LD.n), (SO.h, SO.n), (GPc.h, GPc.n), (LDG.h, LDG.n)]
        for sec in (block.sync, block.tensor, block.vector, block.scalar, block.gpsimd):
            def f(eng, waits=waits):
                for h, n in waits:
                    if n > 0:
                        eng.wait_ge(h, n)
            sec(f)

    last_mrow_gp = [0]
    last_mrec_dv = [0]

    def finish_max_send(accs, cc_name):
        assert len(accs) == 1
        acc = accs[0]
        pe_mark = [0]

        def f_pe(t):
            t.wait_ge(DVc.h, DVc.n)
            t.transpose(psS1[0:1, 0:128], acc[:, :], identf[:, :]).then_inc(PEc.h, 1)
            PEc.n += 1
            pe_mark[0] = PEc.n
        block.tensor(f_pe)

        dvm = [0]

        def f_dv(v):
            v.wait_ge(PEc.h, pe_mark[0])
            v.tensor_reduce(rows[0:1, 3600:3601], psS1[0:1, 0:128], AX, ALU.max).then_inc(DVc.h, 1)
            DVc.n += 1
            dvm[0] = DVc.n
        block.vector(f_dv)

        pe2 = [0]

        def f_pe2(t):
            t.wait_ge(DVc.h, dvm[0])
            t.matmul(psS2[:, 0:2], onesrow[:, :], rows[0:1, 3600:3602],
                     start=True, stop=True).then_inc(PEc.h, 1)
            PEc.n += 1
            pe2[0] = PEc.n
        block.tensor(f_pe2)

        dvm2 = [0]

        def f_dv2(v):
            v.wait_ge(PEc.h, pe2[0])
            v.wait_ge(GPc.h, last_mrow_gp[0])
            v.tensor_copy(mg128[:, 0:1], psS2[:, 0:1]).then_inc(DVc.h, 1)
            DVc.n += 1
            dvm2[0] = DVc.n
        block.vector(f_dv2)

        def f_gp(gp):
            gp.wait_ge(CCc.h, CCc.n)
            gp.wait_ge(DVc.h, dvm2[0])
            gp.dma_start(out=cc_i[cc_name], in_=mg128[:, 0:1]).then_inc(GPc.h, 16)
            GPc.n += 16
            last_mrow_gp[0] = GPc.n
            gp.wait_ge(GPc.h, GPc.n)
            gp.collective_compute("AllReduce", ALU.max, replica_groups=[list(range(8))],
                                  ins=[cc_i[cc_name]], outs=[cc_o[cc_name]]).then_inc(CCc.h, 1)
            CCc.n += 1
        block.gpsimd(f_gp)
        return CCc.n

    def recv_scales(cc_name, cc_val, n_scl, scl_cols, qmax_recips):
        assert n_scl == 1
        gpm = [0]

        def r_gp(gp):
            gp.wait_ge(CCc.h, cc_val)
            gp.wait_ge(DVc.h, last_mrec_dv[0])
            gp.dma_start(out=mr128[:, 0:1], in_=cc_o[cc_name]).then_inc(GPc.h, 16)
            GPc.n += 16
            gpm[0] = GPc.n
        block.gpsimd(r_gp)

        def r_dv(v):
            v.wait_ge(GPc.h, gpm[0])
            v.tensor_scalar(sclbc[:, scl_cols[0]:scl_cols[0] + 1], mr128[:, 0:1],
                            qmax_recips[0], 1e-12, ALU.mult, ALU.max)
            VSP(v)
            v.reciprocal(sclbc[:, scl_cols[0] + 1:scl_cols[0] + 2],
                         sclbc[:, scl_cols[0]:scl_cols[0] + 1]).then_inc(DVc.h, 1)
            DVc.n += 1
            last_mrec_dv[0] = DVc.n
        block.vector(r_dv)
        return DVc.n

    with (
        nc.sbuf_tensor("t_wf", [128, 3072], F32) as t_wfg,
        nc.sbuf_tensor("t_dq1", [128, 3072], F32) as t1g,
        nc.sbuf_tensor("t_dqi", [128, 3072], I32) as tig,
        nc.sbuf_tensor("t_dqs", [128, 96], F32) as tsg,
        nc.sbuf_tensor("t_dqr", [128, 96], F32) as trg,
        nc.sbuf_tensor("t_dqb", [128, 3072], BF16) as tbg,
        nc.sbuf_tensor("t_stage", [128, 3072], BF16) as stgg,
        nc.sbuf_tensor("bvrow", [1, D], F32) as bvrowg,
    ):
        bvrow = bvrowg[:, :]
        t_wf, t1, ti, ts_, tr_, tb, stg = (t_wfg[:, :], t1g[:, :], tig[:, :], tsg[:, :],
                                           trg[:, :], tbg[:, :], stgg[:, :])

        def s0_sync(sync):
            for dst, src in [(identb, identb_d), (identf, identf_d),
                             (onesrow, onesrow_d), (onescol, onescol_d),
                             (maskt, mask_d), (flagr, flagr_d), (flags, flags_d),
                             (bqk_s, bqk), (bproj_s, bproj_d), (lng_s, lng_d),
                             (lnb_s, lnb_d), (bvrow, bv_row)]:
                sync.dma_start(out=dst[:, :], in_=src).then_inc(LD.h, 16)
                LD.n += 16
        block.sync(s0_sync)
        ld_consts = LD.n

        def s0_dv(v):
            v.memset(mg128[:, :], 0.0)
            v.memset(dmy0[:, :], 0.0)
            v.memset(dmy1[:, :], 0.0)
            v.memset(mr128[:, :], 0.0)
            v.memset(sclbc[:, :], 0.0)
            v.memset(rows[:, :], 0.0)
            v.memset(m1acc[:, :], 0.0)
            v.memset(m2acc[:, :], 0.0)
            v.memset(m3acc[:, :], 0.0)
            v.memset(m4acc[:, :], 0.0)
            v.memset(m5acc[:, :], 0.0)
            v.memset(ksum[:, :], 0.0)
            v.memset(S[:, :], 0.0).then_inc(DVc.h, 1)
            DVc.n += 1
        block.vector(s0_dv)

        def s0_pe(t):
            t.wait_ge(LD.h, ld_consts)
            t.matmul(psS1[:, :], onesrow[:, :], bvrow[0:1, 0:512], start=True, stop=True)
            t.matmul(psS2[:, :], onesrow[:, :], bvrow[0:1, 512:1024], start=True, stop=True).then_inc(PEc.h, 1)
            PEc.n += 1
        block.tensor(s0_pe)
        pe_bv = PEc.n

        def s0_dv2(v):
            v.wait_ge(PEc.h, pe_bv)
            v.tensor_copy(bvtile[:, 0:512], psS1[:, :])
            v.tensor_copy(bvtile[:, 512:1024], psS2[:, :]).then_inc(DVc.h, 1)
            DVc.n += 1
        block.vector(s0_dv2)

        def dequant3(src_ap, n_rowchunks, rowlen, store_fn):
            nblk = rowlen // 32
            nhc = rowlen // 128
            for oc in range(n_rowchunks):
                ldm = [0]
                so_pre = SO.n
                dv_pre = DVc.n

                def q_sync(sync, oc=oc, ldm=ldm, so_pre=so_pre, dv_pre=dv_pre):
                    sync.wait_ge(SO.h, so_pre)
                    sync.wait_ge(DVc.h, dv_pre)
                    sync.dma_start(out=t_wf[:, 0:rowlen], in_=src_ap[oc * 128:(oc + 1) * 128, :]).then_inc(LD.h, 16)
                    LD.n += 16
                    ldm[0] = LD.n
                block.sync(q_sync)

                def q_dv(v, ldm=ldm):
                    v.wait_ge(LD.h, ldm[0])
                    w3 = t_wf[:, 0:rowlen].rearrange("p (b c) -> p b c", c=32)
                    v.tensor_reduce(ts_[:, 0:nblk], w3, AX, ALU.max, apply_absolute_value=True)
                    VSP(v)
                    v.tensor_scalar(ts_[:, 0:nblk], ts_[:, 0:nblk], R7, 1e-12, ALU.mult, ALU.max)
                    VSP(v)
                    v.reciprocal(tr_[:, 0:nblk], ts_[:, 0:nblk])
                    VSP(v)
                    rb = tr_[:, 0:nblk].unsqueeze(2).broadcast_to([128, nblk, 32])
                    v.tensor_tensor(t1[:, 0:rowlen].rearrange("p (b c) -> p b c", c=32), w3, rb, ALU.mult)
                    v.tensor_scalar(ti[:, 0:rowlen], t1[:, 0:rowlen], 0.0, None, ALU.add).then_inc(DVc.h, 1)
                    DVc.n += 1
                block.vector(q_dv)
                dvq = DVc.n

                def q_ac(a):
                    a.wait_ge(DVc.h, dvq)
                    a.activation(t1[:, 0:rowlen], ti[:, 0:rowlen], AF.Copy).then_inc(ACc.h, 1)
                    ACc.n += 1
                block.scalar(q_ac)
                acq = ACc.n

                def q_dv2(v):
                    v.wait_ge(ACc.h, acq)
                    sbb = ts_[:, 0:nblk].unsqueeze(2).broadcast_to([128, nblk, 32])
                    v.tensor_tensor(tb[:, 0:rowlen].rearrange("p (b c) -> p b c", c=32),
                                    t1[:, 0:rowlen].rearrange("p (b c) -> p b c", c=32), sbb, ALU.mult).then_inc(DVc.h, 1)
                    DVc.n += 1
                block.vector(q_dv2)
                dvq2 = DVc.n
                pe_marks = []
                dv_tr_base = DVc.n

                def q_pe(t):
                    t.wait_ge(DVc.h, dvq2)
                    for h in range(nhc):
                        slot = h % 2
                        if h >= 2:
                            t.wait_ge(DVc.h, dv_tr_base + h - 1)
                        t.transpose(psT[:, slot * 1024:slot * 1024 + 128], tb[:, h * 128:(h + 1) * 128], identb[:, :]).then_inc(PEc.h, 1)
                        PEc.n += 1
                        pe_marks.append(PEc.n)
                block.tensor(q_pe)

                def q_dv3(v):
                    for h in range(nhc):
                        v.wait_ge(PEc.h, pe_marks[h])
                        v.tensor_copy(stg[:, h * 128:(h + 1) * 128], psT[:, (h % 2) * 1024:(h % 2) * 1024 + 128]).then_inc(DVc.h, 1)
                        DVc.n += 1
                block.vector(q_dv3)
                dvq3 = DVc.n

                def q_sync2(sync, oc=oc):
                    sync.wait_ge(DVc.h, dvq3)
                    store_fn(sync, oc)
                block.sync(q_sync2)

        def store_up(scratch):
            def f(sync, oc):
                sync.dma_start(out=scratch[oc, :, :, :].rearrange("h p c -> p h c"),
                               in_=stg[:, 0:1024].rearrange("p (h c) -> p h c", c=128)).then_inc(SO.h, 16)
                SO.n += 16
            return f

        def store_dn(sync, hc):
            sync.dma_start(out=s_wdnT[hc, :, :, :].rearrange("i p c -> p i c"),
                           in_=stg[:, 0:3072].rearrange("p (i c) -> p i c", c=128)).then_inc(SO.h, 16)
            SO.n += 16

        dequant3(wup_d, IC, 1024, store_up(s_wupT))
        dequant3(wgf_d, IC, 1024, store_up(s_wgtT))
        dequant3(wdn_d, K8, 3072, store_dn)

    if dbg:
        def s0_dbg(sync):
            sync.wait_ge(SO.h, SO.n)
            sync.dma_start(out=dbg_out["d_wup0"], in_=s_wupT[0, :, :, :]).then_inc(SO.h, 16)
            SO.n += 16
        block.sync(s0_dbg)
    barrier()
    if stop_after == "s0":
        _finish()
        return nc

    with (
        nc.sbuf_tensor("xb0", [128, K8 * TW], F32) as xb0g,
        nc.sbuf_tensor("xb16", [128, K8 * TW], BF16) as xb16g,
        nc.sbuf_tensor("xsq16", [128, K8 * TW], BF16) as xsq16g,
        nc.sbuf_tensor("xnb", [128, K8 * TW], F32) as xnbg,
        nc.sbuf_tensor("xnw", [128, K8 * TW], F32) as xnwg,
        nc.sbuf_tensor("xn2b", [128, K8 * TW], BF16) as xn2bg,
        nc.sbuf_tensor("mtmp", [128, 1], F32) as mtmpg,
    ):
        xb = [xb0g[:, :], xb0g[:, :]]
        xb16, xsq16, xnb, xnw, xn2b, mtmp = (xb16g[:, :], xsq16g[:, :], xnbg[:, :],
                                             xnwg[:, :], xn2bg[:, :], mtmpg[:, :])
        dv_free = [DVc.n, DVc.n]
        ac_free = [ACc.n, ACc.n]
        prev_pe3 = [PEc.n]
        prev_dv7 = [DVc.n]
        prev_ac5 = [ACc.n]
        prev_so = [SO.n]

        def newton_rsqrt(v):
            ve = rows[0:1, 1536:2048]
            s_ = rows[0:1, 1024:1536]
            tmp = rows[0:1, 3072:3584]
            v.reciprocal(rows[0:1, 2560:3072], s_)
            return v.tensor_copy(rows[0:1, 2048:2560], rows[0:1, 0:512])

        for g in range(G):
            p = g % 2
            t0 = g * TW
            ldm = [0]
            dv_tile_pre = DVc.n
            dvp, acp = max(dv_free), max(ac_free)

            def p1_sync(sync, p=p, t0=t0, ldm=ldm, dvp=dvp, acp=acp):
                sync.wait_ge(DVc.h, dvp)
                sync.wait_ge(ACc.h, acp)
                sync.dma_start(
                    out=xb[p].rearrange("p (k t) -> p k t", k=K8),
                    in_=xfm.rearrange("(k p) t -> p k t", p=128)[:, :, t0:t0 + TW],
                ).then_inc(LD.h, 16)
                LD.n += 16
                ldm[0] = LD.n
            block.sync(p1_sync)

            pp3 = prev_pe3[0]

            def p1_dv1(v, p=p, ldm=ldm, pp3=pp3):
                v.wait_ge(LD.h, ldm[0])
                v.wait_ge(PEc.h, pp3)
                v.tensor_copy(xb16, xb[p]).then_inc(DVc.h, 1)
                DVc.n += 1
            block.vector(p1_dv1)
            dv1 = DVc.n

            def p1_ac1(a, p=p, ldm=ldm, pp3=pp3):
                a.wait_ge(LD.h, ldm[0])
                a.wait_ge(PEc.h, pp3)
                a.activation(xsq16, xb[p], AF.Square).then_inc(ACc.h, 1)
                ACc.n += 1
            block.scalar(p1_ac1)
            ac1 = ACc.n
            pd7 = dv_tile_pre

            def p1_pe1(t, pd7=pd7):
                t.wait_ge(DVc.h, dv1)
                t.wait_ge(ACc.h, ac1)
                t.wait_ge(DVc.h, pd7)
                for k in range(K8):
                    t.matmul(psS1[0:1, :], onescol[:, :], xb16[:, k * TW:(k + 1) * TW],
                             start=(k == 0), stop=(k == K8 - 1))
                for k in range(K8):
                    mm = t.matmul(psS2[0:1, :], onescol[:, :], xsq16[:, k * TW:(k + 1) * TW],
                                  start=(k == 0), stop=(k == K8 - 1))
                mm.then_inc(PEc.h, 1)
                PEc.n += 1
            block.tensor(p1_pe1)
            pe1 = PEc.n
            prev_pe3[0] = pe1

            if p1a_cut < 2:
                continue

            def p1_dv2(v):
                v.wait_ge(PEc.h, pe1)
                v.tensor_scalar(rows[0:1, 0:512], psS1[0:1, :], R1024, None, ALU.mult)
                v.tensor_scalar(rows[0:1, 512:1024], psS2[0:1, :], R1024, None, ALU.mult)
                VSP(v)
                v.scalar_tensor_tensor(rows[0:1, 1024:1536], rows[0:1, 0:512], -1.0, rows[0:1, 0:512], ALU.mult, ALU.mult)
                VSP(v)
                v.scalar_tensor_tensor(rows[0:1, 1536:2048], rows[0:1, 512:1024], LN_EPS, rows[0:1, 1024:1536], ALU.add, ALU.add).then_inc(DVc.h, 1)
                DVc.n += 1
            block.vector(p1_dv2)
            dv2 = DVc.n

            if p1a_cut < 3:
                continue

            def p1_ac2(a):
                a.wait_ge(DVc.h, dv2)
                a.activation(rows[0:1, 1024:1536], rows[0:1, 1536:2048], AF.Sqrt).then_inc(ACc.h, 1)
                ACc.n += 1
            block.scalar(p1_ac2)
            ac2 = ACc.n

            def p1_dv3(v):
                v.wait_ge(ACc.h, ac2)
                newton_rsqrt(v).then_inc(DVc.h, 1)
                DVc.n += 1
            block.vector(p1_dv3)
            dv3 = DVc.n

            def p1_pe2(t):
                t.wait_ge(DVc.h, dv3)
                t.matmul(psS1[:, :], onesrow[:, :], rows[0:1, 2048:2560], start=True, stop=True)
                t.matmul(psS2[:, :], onesrow[:, :], rows[0:1, 2560:3072], start=True, stop=True).then_inc(PEc.h, 1)
                PEc.n += 1
            block.tensor(p1_pe2)
            pe2 = PEc.n
            pa5 = prev_ac5[0]
            pso = prev_so[0]
            if p1a_cut < 4:
                continue

            def p1_dv4(v, p=p, pa5=pa5, pso=pso):
                v.wait_ge(PEc.h, pe2)
                v.wait_ge(ACc.h, pa5)
                v.wait_ge(SO.h, pso)
                for k in range(K8):
                    v.tensor_tensor(xnw[:, k * TW:(k + 1) * TW], xb[p][:, k * TW:(k + 1) * TW], psS1[:, :], ALU.subtract)
                for k in range(K8):
                    v.tensor_tensor(xnb[:, k * TW:(k + 1) * TW], xnw[:, k * TW:(k + 1) * TW], psS2[:, :], ALU.mult)
                v.tensor_reduce(mtmp, xnb, AX, ALU.max, apply_absolute_value=True)
                VSP(v)
                v.tensor_tensor(m1acc[:, :], m1acc[:, :], mtmp, ALU.max)
                v.tensor_copy(xb16, xnb).then_inc(DVc.h, 1)
                DVc.n += 1
            block.vector(p1_dv4)
            dv4 = DVc.n

            if p1a_cut < 5:
                dv_free[p] = dv4
                continue

            def p1_ac3(a):
                a.wait_ge(DVc.h, dv4)
                a.activation(xsq16, xnb, AF.Square).then_inc(ACc.h, 1)
                ACc.n += 1
            block.scalar(p1_ac3)
            ac3 = ACc.n

            def p1_pe3(t):
                t.wait_ge(ACc.h, ac3)
                for k in range(K8):
                    t.matmul(psS1[0:1, :], onescol[:, :], xb16[:, k * TW:(k + 1) * TW],
                             start=(k == 0), stop=(k == K8 - 1))
                for k in range(K8):
                    mm = t.matmul(psS2[0:1, :], onescol[:, :], xsq16[:, k * TW:(k + 1) * TW],
                                  start=(k == 0), stop=(k == K8 - 1))
                mm.then_inc(PEc.h, 1)
                PEc.n += 1
            block.tensor(p1_pe3)
            pe3 = PEc.n
            prev_pe3[0] = pe3

            def p1_dv5(v):
                v.wait_ge(PEc.h, pe3)
                v.tensor_scalar(rows[0:1, 0:512], psS1[0:1, :], R1024, None, ALU.mult)
                v.tensor_scalar(rows[0:1, 512:1024], psS2[0:1, :], R1024, None, ALU.mult)
                VSP(v)
                v.scalar_tensor_tensor(rows[0:1, 1024:1536], rows[0:1, 0:512], -1.0, rows[0:1, 0:512], ALU.mult, ALU.mult)
                VSP(v)
                v.scalar_tensor_tensor(rows[0:1, 1536:2048], rows[0:1, 512:1024], LN_EPS, rows[0:1, 1024:1536], ALU.add, ALU.add).then_inc(DVc.h, 1)
                DVc.n += 1
            block.vector(p1_dv5)
            dv5 = DVc.n

            if p1a_cut < 6:
                dv_free[p] = dv4
                continue

            def p1_ac4(a):
                a.wait_ge(DVc.h, dv5)
                a.activation(rows[0:1, 1024:1536], rows[0:1, 1536:2048], AF.Sqrt).then_inc(ACc.h, 1)
                ACc.n += 1
            block.scalar(p1_ac4)
            ac4 = ACc.n

            def p1_dv6(v):
                v.wait_ge(ACc.h, ac4)
                newton_rsqrt(v).then_inc(DVc.h, 1)
                DVc.n += 1
            block.vector(p1_dv6)
            dv6 = DVc.n

            def p1_pe4(t):
                t.wait_ge(DVc.h, dv6)
                t.matmul(psS1[:, :], onesrow[:, :], rows[0:1, 2048:2560], start=True, stop=True)
                t.matmul(psS2[:, :], onesrow[:, :], rows[0:1, 2560:3072], start=True, stop=True).then_inc(PEc.h, 1)
                PEc.n += 1
            block.tensor(p1_pe4)
            pe4 = PEc.n

            if p1a_cut < 7:
                dv_free[p] = dv4
                continue

            def p1_dv7(v):
                v.wait_ge(PEc.h, pe4)
                for k in range(K8):
                    v.tensor_tensor(xnw[:, k * TW:(k + 1) * TW], xnb[:, k * TW:(k + 1) * TW], psS1[:, :], ALU.subtract)
                last = None
                for k in range(K8):
                    last = v.tensor_tensor(xnw[:, k * TW:(k + 1) * TW], xnw[:, k * TW:(k + 1) * TW], psS2[:, :], ALU.mult)
                last.then_inc(DVc.h, 1)
                DVc.n += 1
            block.vector(p1_dv7)
            dv7 = DVc.n
            prev_dv7[0] = dv7
            dv_free[p] = dv4

            def p1_ac5(a):
                a.wait_ge(DVc.h, dv7)
                last = None
                for k in range(K8):
                    last = a.activation(xn2b[:, k * TW:(k + 1) * TW], xnw[:, k * TW:(k + 1) * TW], AF.Identity,
                                        bias=lnb_s[:, k:k + 1], scale=lng_s[:, k:k + 1])
                last.then_inc(ACc.h, 1)
                ACc.n += 1
            block.scalar(p1_ac5)
            ac5 = ACc.n
            prev_ac5[0] = ac5
            ac_free[p] = ac1

            def p1_sync2(sync, g=g, t0=t0):
                sync.wait_ge(DVc.h, dv4)
                sync.dma_start(out=s_xn[:, :, t0:t0 + TW].rearrange("k p t -> p k t"),
                               in_=xnb.rearrange("p (k t) -> p k t", k=K8)).then_inc(SO.h, 16)
                SO.n += 16
                sync.wait_ge(ACc.h, ac5)
                sync.dma_start(out=s_xn2[:, :, t0:t0 + TW].rearrange("k p t -> p k t"),
                               in_=xn2b.rearrange("p (k t) -> p k t", k=K8)).then_inc(SO.h, 16)
                SO.n += 16
                if dbg and g == 0:
                    sync.dma_start(out=dbg_out["d_xn"].rearrange("k p t -> p k t"), in_=xnb.rearrange("p (k t) -> p k t", k=K8)).then_inc(SO.h, 16)
                    SO.n += 16
                    sync.dma_start(out=dbg_out["d_xn2"].rearrange("k p t -> p k t"), in_=xn2b.rearrange("p (k t) -> p k t", k=K8)).then_inc(SO.h, 16)
                    SO.n += 16
            block.sync(p1_sync2)
            prev_so[0] = SO.n

        if stop_after == "p1a_tiles":
            _finish()
            return nc
        cc_m1 = finish_max_send([m1acc], "m1")
        dv_after_m1 = DVc.n
    p1a_done_so = SO.n
    if stop_after == "p1a":
        _finish()
        return nc

    with (
        nc.sbuf_tensor("a_xn2", [128, K8 * TW], BF16) as a_xn2g,
        nc.sbuf_tensor("a_kfm", [128, K8 * TW], BF16) as a_kfmg,
        nc.sbuf_tensor("a_ktm", [128, 4 * D], BF16) as a_ktmg,
        nc.sbuf_tensor("a_vtm", [128, 4 * D], BF16) as a_vtmg,
        nc.sbuf_tensor("a_gat", [128, 2 * TW], F32) as a_gatg,
        nc.sbuf_tensor("a_kb", [128, TW], F32) as a_kbg,
        nc.sbuf_tensor("a_kg", [128, TW], F32) as a_kgg,
        nc.sbuf_tensor("a_km", [128, TW], F32) as a_kmg,
        nc.sbuf_tensor("a_ke", [128, TW], F32) as a_keg,
        nc.sbuf_tensor("a_red", [128, 8], F32) as a_redg,
        nc.sbuf_tensor("a_qfm", [128, K8 * TW], BF16) as a_qfmg,
        nc.sbuf_tensor("a_qt", [128, TW], F32) as a_qtg,
        nc.sbuf_tensor("a_qm", [128, TW], F32) as a_qmg,
        nc.sbuf_tensor("a_qe", [128, TW], F32) as a_qeg,
        nc.sbuf_tensor("W1a", [128, 16384], BF16) as W1ag,
        nc.sbuf_tensor("W2a", [128, 8192], BF16) as W2ag,
        nc.sbuf_tensor("WQa", [128, 8192], BF16) as WQag,
        nc.sbuf_tensor("pa_wf", [128, 3072], F32) as pa_wfg,
    ):
        a_xn2, a_kfm, a_ktm, a_vtm = a_xn2g[:, :], a_kfmg[:, :], a_ktmg[:, :], a_vtmg[:, :]
        a_gat, a_kb, a_kg, a_km, a_ke, a_red = (a_gatg[:, :], a_kbg[:, :], a_kgg[:, :],
                                                a_kmg[:, :], a_keg[:, :], a_redg[:, :])
        a_qfm, a_qt, a_qm, a_qe = a_qfmg[:, :], a_qtg[:, :], a_qmg[:, :], a_qeg[:, :]

        W1 = W1ag[:, :]
        W2 = W2ag[:, :]
        WQ = WQag[:, :]
        pa_wf = pa_wfg[:, :]
        for k in range(K8):
            ldw = [0]
            acp = ACc.n

            def s0w_sync(sync, k=k, ldw=ldw, acp=acp):
                sync.wait_ge(ACc.h, acp)
                sync.dma_start(out=pa_wf, in_=wqkv[k * 128:(k + 1) * 128, :]).then_inc(LD.h, 16)
                LD.n += 16
                ldw[0] = LD.n
            block.sync(s0w_sync)

            def s0w_ac(a, k=k, ldw=ldw):
                a.wait_ge(LD.h, ldw[0])
                a.activation(WQ[:, k * 1024:(k + 1) * 1024], pa_wf[:, 0:1024], AF.Copy)
                a.activation(W1[:, k * 2048:k * 2048 + 1024], pa_wf[:, 1024:2048], AF.Copy)
                a.activation(W1[:, k * 2048 + 1024:k * 2048 + 2048], pa_wf[:, 2048:3072], AF.Copy).then_inc(ACc.h, 1)
                ACc.n += 1
            block.scalar(s0w_ac)
        for k in range(K8):
            ldw = [0]
            acp = ACc.n

            def s0g_sync(sync, k=k, ldw=ldw, acp=acp):
                sync.wait_ge(ACc.h, acp)
                sync.dma_start(out=pa_wf[:, 0:1024], in_=wgate_a[k * 128:(k + 1) * 128, :]).then_inc(LD.h, 16)
                LD.n += 16
                ldw[0] = LD.n
            block.sync(s0g_sync)

            def s0g_ac(a, k=k, ldw=ldw):
                a.wait_ge(LD.h, ldw[0])
                a.activation(W2[:, k * 1024:(k + 1) * 1024], pa_wf[:, 0:1024], AF.Copy).then_inc(ACc.h, 1)
                ACc.n += 1
            block.scalar(s0g_ac)
        ac_wcast = ACc.n

        for g in range(G):
            t0 = g * TW
            ldm = [0]
            dv_pre, ac_pre, pe_pre, so_pre = DVc.n, ACc.n, PEc.n, SO.n

            def pa_sync(sync, t0=t0, ldm=ldm, dv_pre=dv_pre, pe_pre=pe_pre, so_pre=so_pre):
                sync.wait_ge(DVc.h, dv_pre)
                sync.wait_ge(PEc.h, pe_pre)
                sync.wait_ge(SO.h, so_pre)
                sync.dma_start(out=a_xn2.rearrange("p (k t) -> p k t", k=K8),
                               in_=s_xn2[:, :, t0:t0 + TW].rearrange("k p t -> p k t")).then_inc(LD.h, 16)
                LD.n += 16
                ldm[0] = LD.n
            block.sync(pa_sync)

            ac_gate_base = ACc.n
            dv_kb_base = DVc.n
            pe_gate, pe_khat = [], []

            def pa_pe1(t, ldm=ldm, ac_gate_base=ac_gate_base, dv_kb_base=dv_kb_base):
                t.wait_ge(LD.h, ldm[0])
                t.wait_ge(ACc.h, max(ac_wcast, 0))
                for n in range(K8):
                    if n >= 1:
                        t.wait_ge(ACc.h, ac_gate_base + 2 * n - 1)
                    for k in range(K8):
                        mm = t.matmul(psA[:, :], W2[:, k * 1024 + n * 128:k * 1024 + (n + 1) * 128],
                                      a_xn2[:, k * TW:(k + 1) * TW], start=(k == 0), stop=(k == K8 - 1))
                    mm.then_inc(PEc.h, 1)
                    PEc.n += 1
                    pe_gate.append(PEc.n)
                    if n >= 1:
                        t.wait_ge(DVc.h, dv_kb_base + 2 * n - 1)
                    for k in range(K8):
                        mm = t.matmul(psB[:, :], W1[:, k * 2048 + n * 128:k * 2048 + (n + 1) * 128],
                                      a_xn2[:, k * TW:(k + 1) * TW], start=(k == 0), stop=(k == K8 - 1))
                    mm.then_inc(PEc.h, 1)
                    PEc.n += 1
                    pe_khat.append(PEc.n)
            block.tensor(pa_pe1)

            ac_chain_base = ACc.n

            def pa_ac1(a, ac_chain_base=ac_chain_base, dv_kb_base=dv_kb_base):
                for n in range(K8):
                    a.wait_ge(PEc.h, pe_gate[n])
                    if n >= 2:
                        a.wait_ge(DVc.h, dv_kb_base + 2 * n - 3)
                    a.activation(a_gat[:, (n % 2) * TW:(n % 2 + 1) * TW], psA[:, :], AF.Sigmoid,
                                 bias=bqk_s[:, 16 + n:17 + n]).then_inc(ACc.h, 1)
                    ACc.n += 1
                    a.wait_ge(DVc.h, dv_kb_base + 2 * n + 1)
                    a.activation(a_ke, a_km, AF.Exp).then_inc(ACc.h, 1)
                    ACc.n += 1
            block.scalar(pa_ac1)

            def pa_dv1(v, ac_chain_base=ac_chain_base):
                for n in range(K8):
                    v.wait_ge(PEc.h, pe_khat[n])
                    v.wait_ge(ACc.h, ac_chain_base + 2 * n + 1)
                    v.tensor_scalar(a_kb, psB[:, :], bqk_s[:, 8 + n:9 + n], None, ALU.add)
                    v.tensor_tensor(a_kg, a_kb, a_gat[:, (n % 2) * TW:(n % 2 + 1) * TW], ALU.mult)
                    v.tensor_scalar(a_km, a_kg, 0.0, None, ALU.min).then_inc(DVc.h, 1)
                    DVc.n += 1
                    v.wait_ge(ACc.h, ac_chain_base + 2 * n + 2)
                    v.scalar_tensor_tensor(a_kfm[:, n * TW:(n + 1) * TW], a_kg, 0.0, a_ke, ALU.max, ALU.add).then_inc(DVc.h, 1)
                    DVc.n += 1
            block.vector(pa_dv1)
            dv_kdone = DVc.n

            dv_qe_base = DVc.n
            ac_qexp_base = ACc.n
            pe_qm = []

            def pa_peQ(t, g=g, dv_qe_base=dv_qe_base):
                if g == 0:
                    t.wait_ge(DVc.h, dv_after_m1)
                for n in range(K8):
                    pp = psS1 if n % 2 == 0 else psS2
                    if n >= 2:
                        t.wait_ge(DVc.h, dv_qe_base + (n - 1) * 2)
                    for k in range(K8):
                        mm = t.matmul(pp[:, :], WQ[:, k * 1024 + n * 128:k * 1024 + (n + 1) * 128],
                                      a_xn2[:, k * TW:(k + 1) * TW], start=(k == 0), stop=(k == K8 - 1))
                    mm.then_inc(PEc.h, 1)
                    PEc.n += 1
                    pe_qm.append(PEc.n)
            block.tensor(pa_peQ)

            def pa_dvQ(v, ac_qexp_base=ac_qexp_base):
                for n in range(K8):
                    pp = psS1 if n % 2 == 0 else psS2
                    v.wait_ge(PEc.h, pe_qm[n])
                    v.tensor_scalar(a_qt, pp[:, :], bqk_s[:, n:n + 1], None, ALU.add)
                    v.tensor_scalar(a_qm, a_qt, 0.0, None, ALU.min).then_inc(DVc.h, 1)
                    DVc.n += 1
                    v.wait_ge(ACc.h, ac_qexp_base + n + 1)
                    v.scalar_tensor_tensor(a_qfm[:, n * TW:(n + 1) * TW], a_qt, 0.0, a_qe, ALU.max, ALU.add).then_inc(DVc.h, 1)
                    DVc.n += 1
            block.vector(pa_dvQ)

            def pa_acQ(a, dv_qe_base=dv_qe_base):
                for n in range(K8):
                    a.wait_ge(DVc.h, dv_qe_base + n * 2 + 1)
                    a.activation(a_qe, a_qm, AF.Exp).then_inc(ACc.h, 1)
                    ACc.n += 1
            block.scalar(pa_acQ)
            dv_qdone = DVc.n

            dv_v_base = DVc.n
            pe_v = []

            def pa_pe2(t, dv_v_base=dv_v_base):
                for m in range(8):
                    j, h2 = m // 2, m % 2
                    pp = psC if (m % 2 == 0) else psD
                    if m >= 2:
                        t.wait_ge(DVc.h, dv_v_base + m - 1)
                    for k in range(K8):
                        mm = t.matmul(pp[:, :], a_xn2[:, k * TW + j * 128:k * TW + (j + 1) * 128],
                                      W1[:, k * 2048 + 1024 + h2 * 512:k * 2048 + 1024 + (h2 + 1) * 512],
                                      start=(k == 0), stop=(k == K8 - 1))
                    mm.then_inc(PEc.h, 1)
                    PEc.n += 1
                    pe_v.append(PEc.n)
            block.tensor(pa_pe2)

            def pa_dv2(v):
                for m in range(8):
                    j, h2 = m // 2, m % 2
                    pp = psC if (m % 2 == 0) else psD
                    v.wait_ge(PEc.h, pe_v[m])
                    v.tensor_tensor(a_vtm[:, j * D + h2 * 512:j * D + (h2 + 1) * 512],
                                    pp[:, :], bvtile[:, h2 * 512:(h2 + 1) * 512], ALU.add).then_inc(DVc.h, 1)
                    DVc.n += 1
            block.vector(pa_dv2)
            dv_vdone = DVc.n
            dv_tr_base = DVc.n
            pe_tr = []

            def pa_pe3(t, dv_tr_base=dv_tr_base):
                t.wait_ge(DVc.h, dv_kdone)
                idx = 0
                for n in range(K8):
                    for j in range(4):
                        slot = idx % 2
                        if idx >= 2:
                            t.wait_ge(DVc.h, dv_tr_base + idx - 1)
                        t.transpose(psT[:, slot * 1024:slot * 1024 + 128],
                                    a_kfm[:, n * TW + j * 128:n * TW + (j + 1) * 128], identb[:, :]).then_inc(PEc.h, 1)
                        PEc.n += 1
                        pe_tr.append(PEc.n)
                        idx += 1
            block.tensor(pa_pe3)

            def pa_dv3(v):
                idx = 0
                for n in range(K8):
                    for j in range(4):
                        v.wait_ge(PEc.h, pe_tr[idx])
                        v.tensor_copy(a_ktm[:, j * D + n * 128:j * D + (n + 1) * 128],
                                      psT[:, (idx % 2) * 1024:(idx % 2) * 1024 + 128]).then_inc(DVc.h, 1)
                        DVc.n += 1
                        idx += 1
            block.vector(pa_dv3)
            dv_trdone = DVc.n

            dv_ds_base = DVc.n
            pe_ds = []

            def pa_pe4(t, dv_ds_base=dv_ds_base):
                t.wait_ge(DVc.h, dv_trdone)
                for m in range(16):
                    d, h2 = m // 2, m % 2
                    pp = psC if (m % 2 == 0) else psD
                    if m >= 2:
                        t.wait_ge(DVc.h, dv_ds_base + m - 1)
                    for j in range(4):
                        mm = t.matmul(pp[:, :], a_ktm[:, j * D + d * 128:j * D + (d + 1) * 128],
                                      a_vtm[:, j * D + h2 * 512:j * D + (h2 + 1) * 512],
                                      start=(j == 0), stop=(j == 3))
                    mm.then_inc(PEc.h, 1)
                    PEc.n += 1
                    pe_ds.append(PEc.n)
            block.tensor(pa_pe4)

            def pa_dv4(v):
                for m in range(16):
                    d, h2 = m // 2, m % 2
                    pp = psC if (m % 2 == 0) else psD
                    v.wait_ge(PEc.h, pe_ds[m])
                    sl = S[:, d * D + h2 * 512:d * D + (h2 + 1) * 512]
                    v.tensor_tensor(sl, sl, pp[:, :], ALU.add).then_inc(DVc.h, 1)
                    DVc.n += 1
                for k in range(K8):
                    v.tensor_reduce(a_red[:, k:k + 1], a_kfm[:, k * TW:(k + 1) * TW], AX, ALU.add)
                VSP(v)
                v.tensor_tensor(ksum[:, 0:8], ksum[:, 0:8], a_red[:, 0:8], ALU.add).then_inc(DVc.h, 1)
                DVc.n += 1
            block.vector(pa_dv4)

            def pa_sync2(sync, g=g, t0=t0):
                sync.wait_ge(DVc.h, dv_kdone)
                sync.dma_start(out=s_kfm[:, :, t0:t0 + TW].rearrange("k p t -> p k t"),
                               in_=a_kfm.rearrange("p (k t) -> p k t", k=K8)).then_inc(SO.h, 16)
                SO.n += 16
                sync.wait_ge(DVc.h, dv_qdone)
                sync.dma_start(out=s_qfm[:, :, t0:t0 + TW].rearrange("k p t -> p k t"),
                               in_=a_qfm.rearrange("p (k t) -> p k t", k=K8)).then_inc(SO.h, 16)
                SO.n += 16
                sync.wait_ge(DVc.h, dv_trdone)
                sync.dma_start(out=s_ktm[4 * g:4 * g + 4, :, :].rearrange("j p d -> p j d"),
                               in_=a_ktm.rearrange("p (j d) -> p j d", j=4)).then_inc(SO.h, 16)
                SO.n += 16
                sync.wait_ge(DVc.h, dv_vdone)
                sync.dma_start(out=s_vtm[4 * g:4 * g + 4, :, :].rearrange("j p d -> p j d"),
                               in_=a_vtm.rearrange("p (j d) -> p j d", j=4)).then_inc(SO.h, 16)
                SO.n += 16
                if dbg and g == 0:
                    sync.dma_start(out=dbg_out["d_kfm"].rearrange("k p t -> p k t"), in_=a_kfm.rearrange("p (k t) -> p k t", k=K8)).then_inc(SO.h, 16)
                    SO.n += 16
                    sync.dma_start(out=dbg_out["d_vtm"].rearrange("j p d -> p j d"), in_=a_vtm.rearrange("p (j d) -> p j d", j=4)).then_inc(SO.h, 16)
                    SO.n += 16
            block.sync(pa_sync2)


        def pa_dvS(v):
            v.tensor_scalar(S[:, :], S[:, :], flags[:, 0:1], None, ALU.mult)
            v.tensor_scalar(ksum[:, :], ksum[:, :], flags[:, 0:1], None, ALU.mult).then_inc(DVc.h, 1)
            DVc.n += 1
        block.vector(pa_dvS)
        dv_smask = DVc.n
        if dbg:
            def pa_dbgS(sync):
                sync.wait_ge(DVc.h, dv_smask)
                sync.dma_start(out=dbg_out["d_S"], in_=S[:, :]).then_inc(SO.h, 16)
                SO.n += 16
                sync.dma_start(out=dbg_out["d_ksum"], in_=ksum[:, :]).then_inc(SO.h, 16)
                SO.n += 16
            block.sync(pa_dbgS)

        def pa_gp(gp):
            gp.wait_ge(CCc.h, CCc.n)
            gp.wait_ge(DVc.h, dv_smask)
            gp.dma_start(out=cc_si[:, 0:K8 * D], in_=S[:, :]).then_inc(GPc.h, 16)
            GPc.n += 16
            gp.dma_start(out=cc_si[:, K8 * D:K8 * D + 8], in_=ksum[:, :]).then_inc(GPc.h, 16)
            GPc.n += 16
            gp.wait_ge(GPc.h, GPc.n)
            gp.collective_compute(
                "AllReduce", ALU.add,
                replica_groups=[[0, 1], [2, 3], [4, 5], [6, 7]],
                ins=[cc_si], outs=[cc_so],
            ).then_inc(CCc.h, 1)
            CCc.n += 1
        block.gpsimd(pa_gp)
        cc_state = CCc.n

    dv_s1 = recv_scales("m1", cc_m1, 1, [0], [R32767])
    barrier()
    if stop_after == "passa":
        _finish()
        return nc

    p3_dv_marks = []
    p3_so_marks = []
    p3_ac_marks = []
    _es2 = contextlib.ExitStack()
    XQg = _es2.enter_context(nc.sbuf_tensor("XQ", [128, K8 * TOK], BF16))
    f_ing = _es2.enter_context(nc.sbuf_tensor("f_in", [128, K8 * TW], F32))
    f_row0g = _es2.enter_context(nc.sbuf_tensor("f_row0", [128, TOK], F32))
    f_wu0g = _es2.enter_context(nc.sbuf_tensor("f_wu0", [128, 1024], BF16))
    f_wu1g = _es2.enter_context(nc.sbuf_tensor("f_wu1", [128, 1024], BF16))
    f_redg = _es2.enter_context(nc.sbuf_tensor("f_red", [128, 1], F32))
    p3i0g = _es2.enter_context(nc.sbuf_tensor("p3_in0", [128, 1024], F32))
    p3i1g = _es2.enter_context(nc.sbuf_tensor("p3_in1", [128, 1024], F32))
    p3ig = _es2.enter_context(nc.sbuf_tensor("p3_int", [128, 1024], I32))
    p3ig2 = _es2.enter_context(nc.sbuf_tensor("p3_int2", [128, 1024], I32))
    p3sfg = _es2.enter_context(nc.sbuf_tensor("p3_sf", [128, 1024], F32))
    p3sfg2 = _es2.enter_context(nc.sbuf_tensor("p3_sf2", [128, 1024], F32))
    p3redg = _es2.enter_context(nc.sbuf_tensor("p3_red", [128, 8], F32))
    if True:
        XQ, f_in = XQg[:, :], f_ing[:, :]
        f_row = [f_row0g[:, :], f_row0g[:, :]]
        if TOK >= K8 * TW:
            f_int = f_row0g[:, :].bitcast(I32)[:, 0:K8 * TW]
        else:
            f_int = _es2.enter_context(nc.sbuf_tensor("f_int", [128, K8 * TW], I32))[:, :]
        f_wu = [f_wu0g[:, :], f_wu1g[:, :]]
        f_red = f_redg[:, :]
        p3_in = [p3i0g[:, :], p3i1g[:, :]]
        p3_int = [p3ig[:, :], p3ig2[:, :]]
        p3_gf = [p3i0g[:, :], p3i1g[:, :]]
        p3_sf = [p3sfg[:, :], p3sfg2[:, :]]
        p3_o = [p3ig[:, :].bitcast(F32), p3ig2[:, :].bitcast(F32)]
        p3_red = p3redg[:, :]

        for g in range(G):
            t0 = g * TW
            ldm = [0]
            acp = ACc.n

            def xq_sync(sync, t0=t0, ldm=ldm, acp=acp):
                sync.wait_ge(ACc.h, acp)
                sync.dma_start(out=f_in.rearrange("p (k t) -> p k t", k=K8),
                               in_=s_xn[:, :, t0:t0 + TW].rearrange("k p t -> p k t")).then_inc(LD.h, 16)
                LD.n += 16
                ldm[0] = LD.n
            block.sync(xq_sync)

            def xq_dv(v, ldm=ldm, g=g):
                v.wait_ge(LD.h, ldm[0])
                if g == 0:
                    v.wait_ge(DVc.h, dv_s1)
                v.tensor_scalar(f_int, f_in, sclbc[:, 1:2], None, ALU.mult).then_inc(DVc.h, 1)
                DVc.n += 1
            block.vector(xq_dv)
            dvx = DVc.n

            def xq_ac(a, g=g, t0=t0):
                a.wait_ge(DVc.h, dvx)
                last = None
                for k in range(K8):
                    last = a.activation(XQ[:, k * TOK + t0:k * TOK + t0 + TW],
                                        f_int[:, k * TW:(k + 1) * TW], AF.Copy,
                                        scale=sclbc[:, 0:1])
                last.then_inc(ACc.h, 1)
                ACc.n += 1
            block.scalar(xq_ac)
            if dbg and g == min(G - 1, 1):
                acxq0 = ACc.n

                def xq_dbg(sync, acxq0=acxq0):
                    sync.wait_ge(ACc.h, acxq0)
                    sync.dma_start(out=dbg_out["d_xq"].rearrange("k p t -> p k t"),
                                   in_=XQ.rearrange("p (k t) -> p k t", t=TOK)[:, :, 0:1024]).then_inc(SO.h, 16)
                    SO.n += 16
                block.sync(xq_dbg)
        ac_xq = ACc.n

        def ffn_mm_phase(w_scratch, s_dst, macc, dbg_key, interleave=None):
            pe_gmarks, pe_rowmarks, so_marks, ac_row_marks, f_dv_marks = [], [], [], [], []
            pe_prev, so_prev, dv_prev = PEc.n, SO.n, DVc.n
            for i in range(IC):
                b = i % 2
                ldm = [0]

                def f_sync(sync, i=i, b=b, ldm=ldm):
                    if i >= 2:
                        sync.wait_ge(PEc.h, pe_rowmarks[i - 2])
                        sync.wait_ge(SO.h, so_marks[i - 2])
                    else:
                        sync.wait_ge(PEc.h, pe_prev)
                        sync.wait_ge(SO.h, so_prev)
                        sync.wait_ge(DVc.h, dv_prev)
                    sync.dma_start(out=f_wu[b].rearrange("p (h c) -> p h c", c=128),
                                   in_=w_scratch[i, :, :, :].rearrange("h p c -> p h c")).then_inc(LD.h, 16)
                    LD.n += 16
                    ldm[0] = LD.n
                block.sync(f_sync)

                ac_base_i = ACc.n

                def f_pe(t, i=i, b=b, ldm=ldm, ac_base_i=ac_base_i):
                    t.wait_ge(LD.h, ldm[0])
                    if i == 0:
                        t.wait_ge(ACc.h, ac_xq)
                    for g in range(G):
                        pp = psRot[g % 2]
                        if g < 2:
                            if i > 0:
                                t.wait_ge(ACc.h, ac_base_i)
                        else:
                            t.wait_ge(ACc.h, ac_base_i + g - 1)
                        for h in range(K8):
                            mm = t.matmul(pp[:, :], f_wu[b][:, h * 128:(h + 1) * 128],
                                          XQ[:, h * TOK + g * TW:h * TOK + (g + 1) * TW],
                                          start=(h == 0), stop=(h == K8 - 1))
                        mm.then_inc(PEc.h, 1)
                        PEc.n += 1
                        pe_gmarks.append(PEc.n)
                    pe_rowmarks.append(PEc.n)
                block.tensor(f_pe)

                def f_ac(a, i=i, b=b):
                    for g in range(G):
                        pp = psRot[g % 2]
                        if g == 0 and i >= 1:
                            a.wait_ge(SO.h, so_marks[i - 1])
                            a.wait_ge(DVc.h, f_dv_marks[i - 1])
                        if g == 0 and i == 0:
                            a.wait_ge(SO.h, so_prev)
                            a.wait_ge(DVc.h, dv_prev)
                        a.wait_ge(PEc.h, pe_gmarks[i * G + g])
                        a.activation(f_row[b][:, g * TW:(g + 1) * TW], pp[:, :], AF.Copy).then_inc(ACc.h, 1)
                        ACc.n += 1
                block.scalar(f_ac)
                ac_row_marks.append(ACc.n)
                ac_row = ACc.n

                def f_dv(v, b=b):
                    v.wait_ge(ACc.h, ac_row)
                    v.tensor_reduce(f_red, f_row[b], AX, ALU.max, apply_absolute_value=True)
                    VSP(v)
                    v.tensor_tensor(macc[:, :], macc[:, :], f_red, ALU.max).then_inc(DVc.h, 1)
                    DVc.n += 1
                    f_dv_marks.append(DVc.n)
                block.vector(f_dv)

                def f_sync2(sync, i=i, b=b):
                    sync.wait_ge(ACc.h, ac_row)
                    sync.dma_start(out=s_dst[i, :, :], in_=f_row[b]).then_inc(SO.h, 16)
                    SO.n += 16
                    so_marks.append(SO.n)
                    if dbg and i == 0 and dbg_key:
                        sync.dma_start(out=dbg_out[dbg_key], in_=f_row[b]).then_inc(SO.h, 16)
                        SO.n += 16
                block.sync(f_sync2)
                if interleave is not None and i >= 1:
                    interleave()

        ffn_mm_phase(s_wgtT, s_gt, m3acc, "d_gt0")
        gt_stores = SO.n
        cc_m3 = finish_max_send([m3acc], "m23")
        dv_s3 = recv_scales("m23", cc_m3, 1, [4], [R32767])

        NQ = TOK // 1024
        NP3 = IC * NQ
        p3_idx = [0]

        def emit_p3_pairs(npairs):
            for _ in range(npairs):
                i0 = p3_idx[0]
                if i0 >= NP3:
                    return
                pair = [i0] + ([i0 + 1] if i0 + 1 < NP3 else [])
                ldm = [0]

                def p3_sync(sync, pair=pair, i0=i0, ldm=ldm):
                    for u, idx in enumerate(pair):
                        if idx >= 2:
                            sync.wait_ge(DVc.h, p3_dv_marks[idx - 2])
                        if idx >= 4:
                            sync.wait_ge(SO.h, p3_so_marks[idx - 4])
                        c, q = idx // NQ, idx % NQ
                        sync.dma_start(out=p3_in[u][:, :], in_=s_gt[c, :, q * 1024:(q + 1) * 1024]).then_inc(LD.h, 16)
                        LD.n += 16
                    ldm[0] = LD.n
                block.sync(p3_sync)

                def p3_dv1(v, pair=pair, ldm=ldm, i0=i0):
                    v.wait_ge(LD.h, ldm[0])
                    if i0 == 0:
                        v.wait_ge(DVc.h, dv_s3)
                    for u, idx in enumerate(pair):
                        if idx >= 2:
                            v.wait_ge(ACc.h, p3_ac_marks[idx - 2])
                        v.tensor_scalar(p3_int[u], p3_in[u], sclbc[:, 5:6], None, ALU.mult)
                    v.nop2 = None
                    v.tensor_copy(dmy1[:, :], dmy0[:, :]).then_inc(DVc.h, 1)
                    DVc.n += 1
                block.vector(p3_dv1)
                mk1 = DVc.n

                def p3_ac1(a, pair=pair):
                    a.wait_ge(DVc.h, mk1)
                    last = None
                    for u in range(len(pair)):
                        a.activation(p3_gf[u], p3_int[u], AF.Copy, scale=sclbc[:, 4:5])
                        last = a.activation(p3_sf[u], p3_gf[u], AF.Sigmoid)
                    last.then_inc(ACc.h, 1)
                    ACc.n += 1
                block.scalar(p3_ac1)
                mk2 = ACc.n

                def p3_dv2(v, pair=pair):
                    v.wait_ge(ACc.h, mk2)
                    last = None
                    for u in range(len(pair)):
                        last = v.tensor_scalar(p3_int[u], p3_sf[u], 65536.0, None, ALU.mult)
                    last.then_inc(DVc.h, 1)
                    DVc.n += 1
                block.vector(p3_dv2)
                mk3 = DVc.n

                def p3_ac2(a, pair=pair):
                    a.wait_ge(DVc.h, mk3)
                    last = None
                    for u in range(len(pair)):
                        last = a.activation(p3_sf[u], p3_int[u], AF.Copy, scale=R65536)
                    last.then_inc(ACc.h, 1)
                    ACc.n += 1
                block.scalar(p3_ac2)
                mk4 = ACc.n
                for idx in pair:
                    p3_ac_marks.append(mk4)

                def p3_dv3(v, pair=pair, i0=i0):
                    v.wait_ge(ACc.h, mk4)
                    for u, idx in enumerate(pair):
                        if idx >= 2:
                            v.wait_ge(SO.h, p3_so_marks[idx - 2])
                        v.scalar_tensor_tensor(p3_o[u], p3_sf[u], C65535, p3_gf[u], ALU.min, ALU.mult)
                    for u in range(len(pair)):
                        v.tensor_reduce(p3_red[:, u:u + 1], p3_o[u], AX, ALU.max, apply_absolute_value=True)
                    VSP(v)
                    v.tensor_tensor(m4acc[:, :], m4acc[:, :], p3_red[:, 0:1], ALU.max)
                    if len(pair) > 1:
                        VSP(v)
                        v.tensor_tensor(m4acc[:, :], m4acc[:, :], p3_red[:, 1:2], ALU.max)
                    v.tensor_copy(dmy1[:, :], dmy0[:, :]).then_inc(DVc.h, 1)
                    DVc.n += 1
                block.vector(p3_dv3)
                for idx in pair:
                    p3_dv_marks.append(DVc.n)

                def p3_sync2(sync, pair=pair):
                    sync.wait_ge(DVc.h, p3_dv_marks[-1])
                    for u, idx in enumerate(pair):
                        c, q = idx // NQ, idx % NQ
                        sync.dma_start(out=s_g2[c, :, q * 1024:(q + 1) * 1024], in_=p3_o[u][:, :]).then_inc(SO.h, 16)
                        SO.n += 16
                        p3_so_marks.append(SO.n)
                    if dbg and pair[0] == 0:
                        sync.dma_start(out=dbg_out["d_g20"], in_=p3_o[0][:, :]).then_inc(SO.h, 16)
                        SO.n += 16
                block.sync(p3_sync2)
                p3_idx[0] += len(pair)

        ffn_mm_phase(s_wupT, s_up, m2acc, "d_up0", interleave=lambda: emit_p3_pairs(2))
        up_stores = SO.n
        cc_m2 = finish_max_send([m2acc], "m23")
        dv_s2 = recv_scales("m23", cc_m2, 1, [2], [R32767])

        emit_p3_pairs(NP3)

        cc_m4 = finish_max_send([m4acc], "m4")
        dv_s4 = recv_scales("m4", cc_m4, 1, [6], [R32767])

    barrier()

    _es2.close()
    if stop_after == "p2":
        _finish()
        return nc
    p4_dv_marks = []
    p4_so_marks = []
    p4_idx = [0]
    _esB = contextlib.ExitStack()
    W1bg = _esB.enter_context(nc.sbuf_tensor("W1b", [128, 8192], BF16))
    b_qg = [_esB.enter_context(nc.sbuf_tensor(f"b_q{p}", [128, K8 * TW], BF16)) for p in range(2)]
    b_kfmg = [_esB.enter_context(nc.sbuf_tensor(f"b_kfm{p}", [128, K8 * TW], BF16)) for p in range(2)]
    b_ktmg = _esB.enter_context(nc.sbuf_tensor("b_ktm", [128, 4 * D], BF16))
    b_vtmg = _esB.enter_context(nc.sbuf_tensor("b_vtm", [128, 4 * D], BF16))
    b_yg = _esB.enter_context(nc.sbuf_tensor("b_y", [128, K8 * TW], BF16))
    b_scg = _esB.enter_context(nc.sbuf_tensor("b_sc", [128, 4 * TW], BF16))
    b_drg = _esB.enter_context(nc.sbuf_tensor("b_dr", [128, TW], F32))
    b_prjg = _esB.enter_context(nc.sbuf_tensor("b_prj", [128, K8 * TW], F32))
    b_redg = _esB.enter_context(nc.sbuf_tensor("b_red", [128, 8], F32))
    p4ag = [_esB.enter_context(nc.sbuf_tensor(f"p4_a{u}", [128, 1024], F32)) for u in range(4)]
    p4bg = [_esB.enter_context(nc.sbuf_tensor(f"p4_b{u}", [128, 1024], F32)) for u in range(4)]
    p4ig_ = [_esB.enter_context(nc.sbuf_tensor(f"p4_i{u}", [128, 1024], I32)) for u in range(2)]
    p4redwg = _esB.enter_context(nc.sbuf_tensor("p4_redw", [128, 128], F32))
    if True:
        W1 = W1bg[:, :]
        b_q = [b_qg[0][:, :], b_qg[1][:, :]]
        b_kfm = [b_kfmg[0][:, :], b_kfmg[1][:, :]]
        b_ktm = b_ktmg[:, :]
        b_vtm = b_vtmg[:, :]
        b_y, b_sc, b_dr = b_yg[:, :], b_scg[:, :], b_drg[:, :]
        b_prj, b_red = b_prjg[:, :], b_redg[:, :]
        p4_a = [t[:, :] for t in p4ag]
        p4_b = [t[:, :] for t in p4bg]
        p4_int = [t[:, :] for t in p4ig_]
        p4_intb = p4_int
        p4_g2 = p4_a
        p4_o = [t[:, :].bitcast(F32) for t in p4ig_]
        p4_redw = p4redwg[:, :]
        psTf = psT.bitcast(F32)
        psE, psF = psTf[:, 0:TW], psTf[:, TW:2 * TW]
        psSC = [psE, psF, psS1, psS2]
        psNum = [psA, psB]
        NQ = max(TOK // 1024, 1)
        NP4 = IC * (TOK // 1024)

        for rnd in range(2):
            ldm = [0]
            dvp = DVc.n

            def wc_sync(sync, rnd=rnd, ldm=ldm, dvp=dvp):
                sync.wait_ge(DVc.h, dvp)
                sync.dma_start(out=b_prj.rearrange("p (k c) -> p k c", k=4),
                               in_=wproj.rearrange("(k p) c -> p k c", p=128)[:, 4 * rnd:4 * rnd + 4, 0:1024]).then_inc(LD.h, 16)
                LD.n += 16
                ldm[0] = LD.n
            block.sync(wc_sync)

            def wc_dv(v, rnd=rnd, ldm=ldm):
                v.wait_ge(LD.h, ldm[0])
                last = None
                for kk in range(4):
                    k = 4 * rnd + kk
                    last = v.tensor_copy(W1[:, k * 1024:(k + 1) * 1024],
                                         b_prj[:, kk * 1024:(kk + 1) * 1024])
                last.then_inc(DVc.h, 1)
                DVc.n += 1
            block.vector(wc_dv)
        dv_wqp = DVc.n
        if dbg:
            def dbg_w1(sync):
                sync.wait_ge(DVc.h, dv_wqp)
                sync.dma_start(out=dbg_out["d_W1"], in_=W1[:, :]).then_inc(SO.h, 16)
                SO.n += 16
            block.sync(dbg_w1)

        lds = [0]

        def bi_syncS(sync):
            sync.wait_ge(CCc.h, cc_state)
            sync.dma_start(out=S[:, :], in_=cc_so[:, 0:K8 * D]).then_inc(LD.h, 16)
            LD.n += 16
            sync.dma_start(out=ksum[:, :], in_=cc_so[:, K8 * D:K8 * D + 8]).then_inc(LD.h, 16)
            LD.n += 16
            lds[0] = LD.n
        block.sync(bi_syncS)

        def bi_dvS(v):
            v.wait_ge(LD.h, lds[0])
            v.tensor_scalar(S[:, :], S[:, :], flagr[:, 0:1], None, ALU.mult)
            v.tensor_scalar(ksum[:, :], ksum[:, :], flagr[:, 0:1], None, ALU.mult).then_inc(DVc.h, 1)
            DVc.n += 1
        block.vector(bi_dvS)
        dv_sinit = DVc.n

        def bi_acS(a):
            a.wait_ge(DVc.h, dv_sinit)
            a.activation(Sb[:, :], S[:, :], AF.Copy)
            a.activation(ksum_b[:, :], ksum[:, :], AF.Copy).then_inc(ACc.h, 1)
            ACc.n += 1
        block.scalar(bi_acS)
        ac_sb = [ACc.n]

        ld_tile = [0] * (G + 2)
        pe_num_last_l = [0] * (G + 2)
        dv_supd_l = [0] * (G + 2)
        dv_drt_l = [0] * (G + 2)
        dv_y_l = [0] * (G + 2)
        so_store_l = [0] * (G + 2)
        ac_pr_l = [0] * (G + 2)

        def emit_loads(gq):
            t0q = gq * TW
            wpe = pe_num_last_l[gq - 2] if gq >= 2 else 0
            wdv = dv_supd_l[gq - 2] if gq >= 2 else 0
            p = gq % 2

            def gl(gp, gq=gq, t0q=t0q, wpe=wpe, wdv=wdv, p=p):
                if wpe:
                    gp.wait_ge(PEc.h, wpe)
                if wdv:
                    gp.wait_ge(DVc.h, wdv)
                gp.dma_start(out=b_q[p].rearrange("p (k t) -> p k t", k=K8),
                             in_=s_qfm[:, :, t0q:t0q + TW].rearrange("k p t -> p k t")).then_inc(LDG.h, 16)
                LDG.n += 16
                gp.dma_start(out=b_kfm[p].rearrange("p (k t) -> p k t", k=K8),
                             in_=s_kfm[:, :, t0q:t0q + TW].rearrange("k p t -> p k t")).then_inc(LDG.h, 16)
                LDG.n += 16
            block.gpsimd(gl)
            ld_tile[gq] = LDG.n

        ld2_tile = [0] * (G + 2)

        def emit_loads2(gq):
            t0q = gq * TW
            wpe = pe_num_last_l[gq - 1] if gq >= 1 else 0

            def gl2(gp, gq=gq, t0q=t0q, wpe=wpe):
                if wpe:
                    gp.wait_ge(PEc.h, wpe)
                gp.dma_start(out=b_ktm.rearrange("p (j d) -> p j d", j=4),
                             in_=s_ktm[4 * gq:4 * gq + 4, :, :].rearrange("j p d -> p j d")).then_inc(LDG.h, 16)
                LDG.n += 16
                gp.dma_start(out=b_vtm.rearrange("p (j d) -> p j d", j=4),
                             in_=s_vtm[4 * gq:4 * gq + 4, :, :].rearrange("j p d -> p j d")).then_inc(LDG.h, 16)
                LDG.n += 16
            block.gpsimd(gl2)
            ld2_tile[gq] = LDG.n

        def emit_p4_chunks(nchunks):
            while nchunks > 0:
                i0 = p4_idx[0]
                if i0 >= NP4:
                    return
                pair = [i0] + ([i0 + 1] if i0 + 1 < NP4 else [])
                nchunks -= len(pair)
                ldm = [0]
                sb = ((i0 // 2) % 2) * 2

                def s_(sync, pair=pair, ldm=ldm, sb=sb):
                    for u, idx in enumerate(pair):
                        if idx >= 4:
                            sync.wait_ge(DVc.h, p4_dv_marks[idx - 4])
                            sync.wait_ge(SO.h, p4_so_marks[idx - 4])
                        c, q = idx // NQ, idx % NQ
                        sync.dma_start(out=p4_a[sb + u][:, :], in_=s_g2[c, :, q * 1024:(q + 1) * 1024]).then_inc(LD.h, 16)
                        sync.dma_start(out=p4_b[sb + u][:, :], in_=s_up[c, :, q * 1024:(q + 1) * 1024]).then_inc(LD.h, 16)
                        LD.n += 32
                    ldm[0] = LD.n
                block.sync(s_)

                def a1_(a, pair=pair, ldm=ldm, i0=i0, sb=sb):
                    a.wait_ge(LD.h, ldm[0])
                    if i0 == 0:
                        a.wait_ge(DVc.h, dv_s4)
                    nu = len(pair)

                    def spc():
                        if nu == 1:
                            a.activation(dmy1[:, :], dmy0[:, :], AF.Copy)
                    last = None
                    for u in range(nu):
                        last = a.activation(p4_int[u], p4_a[sb + u], AF.Copy, scale=sclbc[:, 7:8])
                    spc()
                    for u in range(nu):
                        last = a.activation(p4_g2[sb + u], p4_int[u], AF.Copy, scale=sclbc[:, 6:7])
                    spc()
                    for u in range(nu):
                        last = a.activation(p4_intb[u], p4_b[sb + u], AF.Copy, scale=sclbc[:, 3:4])
                    spc()
                    for u in range(nu):
                        last = a.activation(p4_b[sb + u], p4_intb[u], AF.Copy, scale=sclbc[:, 2:3])
                    last.then_inc(ACc.h, 1)
                    ACc.n += 1
                block.scalar(a1_)
                mk4 = ACc.n

                def v3_(v, pair=pair, sb=sb):
                    v.wait_ge(ACc.h, mk4)
                    for u, idx in enumerate(pair):
                        v.tensor_tensor(p4_a[sb + u], p4_g2[sb + u], p4_b[sb + u], ALU.mult)
                    for u, idx in enumerate(pair):
                        v.tensor_reduce(p4_redw[:, idx:idx + 1], p4_a[sb + u], AX, ALU.max,
                                        apply_absolute_value=True).then_inc(DVc.h, 1)
                        DVc.n += 1
                block.vector(v3_)
                for idx in pair:
                    p4_dv_marks.append(DVc.n)

                def s2_(sync, pair=pair, sb=sb):
                    sync.wait_ge(DVc.h, p4_dv_marks[-1])
                    for u, idx in enumerate(pair):
                        c, q = idx // NQ, idx % NQ
                        sync.dma_start(out=s_o[c, :, q * 1024:(q + 1) * 1024], in_=p4_a[sb + u][:, :]).then_inc(SO.h, 16)
                        SO.n += 16
                        p4_so_marks.append(SO.n)
                    if dbg and pair[0] == 0:
                        sync.dma_start(out=dbg_out["d_o0"], in_=p4_a[sb][:, :]).then_inc(SO.h, 16)
                        SO.n += 16
                block.sync(s2_)
                p4_idx[0] += len(pair)

        p4_per_tile = (NP4 + G - 1) // G

        emit_loads(0)
        emit_loads2(0)
        if G > 1:
            emit_loads(1)

        for g in range(G):
            t0 = g * TW
            if g >= 1 and g + 1 < G:
                emit_loads(g + 1)

            pe_sc = []
            w_drt_prev = dv_drt_l[g - 1] if g >= 1 else 0

            def b_pesc(t, g=g, w_drt_prev=w_drt_prev):
                t.wait_ge(LDG.h, ld_tile[g])
                if w_drt_prev:
                    t.wait_ge(DVc.h, w_drt_prev)
                for j in range(4):
                    N = TW - 128 * j
                    pp = psSC[j]
                    for k in range(K8):
                        mm = t.matmul(pp[:, 0:N], b_kfm[g % 2][:, k * TW + j * 128:k * TW + (j + 1) * 128],
                                      b_q[g % 2][:, k * TW + j * 128:k * TW + TW], start=(k == 0), stop=(k == K8 - 1))
                    mm.then_inc(PEc.h, 1)
                    PEc.n += 1
                    pe_sc.append(PEc.n)
            block.tensor(b_pesc)

            def b_dvsc(v):
                for j in range(4):
                    N = TW - 128 * j
                    pp = psSC[j]
                    v.wait_ge(PEc.h, pe_sc[j])
                    v.tensor_tensor(b_sc[:, j * TW:j * TW + N], pp[:, 0:N], maskt[:, 0:N], ALU.mult).then_inc(DVc.h, 1)
                    DVc.n += 1
            block.vector(b_dvsc)
            dv_scdone = DVc.n

            pe_den = [0]

            def b_peden(t, g=g):
                t.wait_ge(DVc.h, dv_scdone)
                t.wait_ge(ACc.h, ac_sb[0])
                for d in range(K8):
                    t.matmul(psS1[0:1, :], ksum_b[:, d:d + 1], b_q[g % 2][:, d * TW:(d + 1) * TW],
                             start=(d == 0), stop=False)
                for j in range(4):
                    N = TW - 128 * j
                    mm = t.matmul(psS1[0:1, 128 * j:TW], onescol[:, :], b_sc[:, j * TW:j * TW + N],
                                  start=False, stop=(j == 3))
                mm.then_inc(PEc.h, 1)
                PEc.n += 1
                pe_den[0] = PEc.n
            block.tensor(b_peden)

            def b_dvden(v):
                v.wait_ge(PEc.h, pe_den[0])
                v.tensor_scalar(rows[0:1, 0:TW], psS1[0:1, :], DEN_EPS, None, ALU.add)
                VSP(v)
                v.reciprocal(rows[0:1, TW:2 * TW], rows[0:1, 0:TW]).then_inc(DVc.h, 1)
                DVc.n += 1
            block.vector(b_dvden)
            dv_den = DVc.n
            if dbg and g == 0:
                def b_dbgden(sync):
                    sync.wait_ge(DVc.h, dv_den)
                    sync.dma_start(out=dbg_out["d_den"], in_=rows[0:1, 0:TW]).then_inc(SO.h, 16)
                    SO.n += 16
                block.sync(b_dbgden)

            pe_denb = [0]

            def b_pednb(t):
                t.wait_ge(DVc.h, dv_den)
                t.matmul(psS2[:, :], onesrow[:, :], rows[0:1, TW:2 * TW], start=True, stop=True).then_inc(PEc.h, 1)
                PEc.n += 1
                pe_denb[0] = PEc.n
            block.tensor(b_pednb)

            def b_dvdnb(v):
                v.wait_ge(PEc.h, pe_denb[0])
                v.tensor_copy(b_dr, psS2[:, :]).then_inc(DVc.h, 1)
                DVc.n += 1
            block.vector(b_dvdnb)
            dv_drt = DVc.n
            dv_drt_l[g] = dv_drt

            dv_ds2_base = DVc.n
            pe_ds2 = []
            w_supd_prev = dv_supd_l[g - 1] if g >= 1 else 0

            def b_peds(t, g=g, w_supd_prev=w_supd_prev, dv_ds2_base=dv_ds2_base):
                t.wait_ge(LDG.h, ld2_tile[g])
                if w_supd_prev:
                    t.wait_ge(DVc.h, w_supd_prev)
                for m in range(16):
                    d, h2 = m // 2, m % 2
                    pp = psC if (m % 2 == 0) else psD
                    if m >= 2:
                        t.wait_ge(DVc.h, dv_ds2_base + m - 1)
                    for j in range(4):
                        mm = t.matmul(pp[:, :], b_ktm[:, j * D + d * 128:j * D + (d + 1) * 128],
                                      b_vtm[:, j * D + h2 * 512:j * D + (h2 + 1) * 512],
                                      start=(j == 0), stop=(j == 3))
                    mm.then_inc(PEc.h, 1)
                    PEc.n += 1
                    pe_ds2.append(PEc.n)
            block.tensor(b_peds)

            def b_dvds(v, g=g):
                v.wait_ge(ACc.h, ac_sb[0])
                for m in range(16):
                    d, h2 = m // 2, m % 2
                    pp = psC if (m % 2 == 0) else psD
                    v.wait_ge(PEc.h, pe_ds2[m])
                    sl = S[:, d * D + h2 * 512:d * D + (h2 + 1) * 512]
                    v.tensor_tensor(sl, sl, pp[:, :], ALU.add).then_inc(DVc.h, 1)
                    DVc.n += 1
                for k in range(K8):
                    v.tensor_reduce(b_red[:, k:k + 1], b_kfm[g % 2][:, k * TW:(k + 1) * TW], AX, ALU.add)
                VSP(v)
                v.tensor_tensor(ksum[:, 0:8], ksum[:, 0:8], b_red[:, 0:8], ALU.add).then_inc(DVc.h, 1)
                DVc.n += 1
            block.vector(b_dvds)
            dv_supd = DVc.n
            dv_supd_l[g] = dv_supd

            dv_num_base = DVc.n
            pe_num = []
            pe_sb_free = [0]
            w_y_prev = dv_y_l[g - 1] if g >= 1 else 0
            w_pr_prev = ac_pr_l[g - 1] if g >= 1 else 0

            def b_penum(t, g=g, dv_num_base=dv_num_base, w_y_prev=w_y_prev, w_pr_prev=w_pr_prev):
                t.wait_ge(DVc.h, dv_scdone)
                t.wait_ge(ACc.h, ac_sb[0])
                if w_y_prev:
                    t.wait_ge(DVc.h, w_y_prev)
                if w_pr_prev:
                    t.wait_ge(ACc.h, w_pr_prev)
                for m in range(K8):
                    pp = psNum[m % 2]
                    if m >= 2:
                        t.wait_ge(DVc.h, dv_num_base + m - 1)
                    for d in range(K8):
                        t.matmul(pp[:, :], Sb[:, d * D + m * 128:d * D + (m + 1) * 128],
                                 b_q[g % 2][:, d * TW:(d + 1) * TW], start=(d == 0), stop=False)
                    for j in range(4):
                        N = TW - 128 * j
                        mm = t.matmul(pp[:, 128 * j:TW], b_vtm[:, j * D + m * 128:j * D + (m + 1) * 128],
                                      b_sc[:, j * TW:j * TW + N], start=False, stop=(j == 3))
                    mm.then_inc(PEc.h, 1)
                    PEc.n += 1
                    pe_num.append(PEc.n)
                pe_sb_free[0] = PEc.n
            block.tensor(b_penum)
            pe_num_last_l[g] = PEc.n

            def b_dvnum(v):
                v.wait_ge(DVc.h, dv_drt)
                for m in range(K8):
                    pp = psNum[m % 2]
                    v.wait_ge(PEc.h, pe_num[m])
                    v.tensor_tensor(b_y[:, m * TW:(m + 1) * TW], pp[:, :], b_dr, ALU.mult).then_inc(DVc.h, 1)
                    DVc.n += 1
            block.vector(b_dvnum)
            dv_ydone = DVc.n
            dv_y_l[g] = dv_ydone

            def b_acSb(a, pe_sb_free=pe_sb_free, dv_supd=dv_supd):
                a.wait_ge(DVc.h, dv_supd)
                a.wait_ge(PEc.h, pe_sb_free[0])
                a.activation(Sb[:, :], S[:, :], AF.Copy)
                a.activation(ksum_b[:, :], ksum[:, :], AF.Copy).then_inc(ACc.h, 1)
                ACc.n += 1
                ac_sb[0] = ACc.n
            block.scalar(b_acSb)

            ac_pr_base = ACc.n
            pe_pr = []

            def b_pepr(t, g=g, ac_pr_base=ac_pr_base, dv_ydone=dv_ydone):
                t.wait_ge(DVc.h, dv_ydone)
                if g == 0:
                    t.wait_ge(DVc.h, dv_wqp)
                for m in range(K8):
                    pp = psNum[m % 2]
                    if m >= 2:
                        t.wait_ge(ACc.h, ac_pr_base + m - 1)
                    for d in range(K8):
                        mm = t.matmul(pp[:, :], W1[:, d * 1024 + m * 128:d * 1024 + (m + 1) * 128],
                                      b_y[:, d * TW:(d + 1) * TW], start=(d == 0), stop=(d == K8 - 1))
                    mm.then_inc(PEc.h, 1)
                    PEc.n += 1
                    pe_pr.append(PEc.n)
            block.tensor(b_pepr)

            w_so_prev = so_store_l[g - 1] if g >= 1 else 0

            def b_acpr(a, w_so_prev=w_so_prev):
                for m in range(K8):
                    pp = psNum[m % 2]
                    a.wait_ge(PEc.h, pe_pr[m])
                    if m < 2 and w_so_prev:
                        a.wait_ge(SO.h, w_so_prev)
                    a.activation(b_prj[:, m * TW:(m + 1) * TW], pp[:, :], AF.Identity,
                                 bias=bproj_s[:, m:m + 1]).then_inc(ACc.h, 1)
                    ACc.n += 1
            block.scalar(b_acpr)
            ac_pr_last = ACc.n
            ac_pr_l[g] = ac_pr_last

            if g + 1 < G:
                emit_loads2(g + 1)

            emit_p4_chunks(p4_per_tile)

            def b_sync2(sync, g=g, t0=t0, ac_pr_last=ac_pr_last):
                sync.wait_ge(ACc.h, ac_pr_last)
                sync.dma_start(out=s_part[:, :, t0:t0 + TW].rearrange("k p t -> p k t"),
                               in_=b_prj.rearrange("p (k t) -> p k t", k=K8)).then_inc(SO.h, 16)
                SO.n += 16
                if dbg and g == 0:
                    sync.dma_start(out=dbg_out["d_part"].rearrange("k p t -> p k t"), in_=b_prj.rearrange("p (k t) -> p k t", k=K8)).then_inc(SO.h, 16)
                    SO.n += 16
                    sync.dma_start(out=dbg_out["d_q"].rearrange("k p t -> p k t"), in_=b_q[0].rearrange("p (k t) -> p k t", k=K8)).then_inc(SO.h, 16)
                    SO.n += 16
                    sync.dma_start(out=dbg_out["d_y"].rearrange("k p t -> p k t"), in_=b_y.rearrange("p (k t) -> p k t", k=K8)).then_inc(SO.h, 16)
                    SO.n += 16
            block.sync(b_sync2)
            so_store_l[g] = SO.n

        emit_p4_chunks(NP4)

        if NP4 > 0:
            def p4_fin(v):
                VSP(v)
                v.tensor_reduce(m5acc[:, :], p4_redw[:, 0:NP4], AX, ALU.max).then_inc(DVc.h, 1)
                DVc.n += 1
            block.vector(p4_fin)
        cc_m5 = finish_max_send([m5acc], "m5")
        dv_s5 = recv_scales("m5", cc_m5, 1, [8], [R32767])
        if dbg:
            def dbg_scl(sync):
                sync.wait_ge(DVc.h, dv_s5)
                sync.dma_start(out=dbg_out["d_scl"], in_=sclbc[:, :]).then_inc(SO.h, 16)
                SO.n += 16
            block.sync(dbg_scl)

    barrier()

    _esB.close()
    if stop_after == "phaseb":
        _finish()
        return nc
    with (
        nc.sbuf_tensor("o_in0", [128, 6 * TW], F32) as oi0g,
        nc.sbuf_tensor("o_in1", [128, 6 * TW], F32) as oi1g,
        nc.sbuf_tensor("o_int", [128, 6 * TW], I32) as ointg,
        nc.sbuf_tensor("OB", [128, IC * TW], BF16) as OBg,
        nc.sbuf_tensor("wdn0", [128, IC * 128], BF16) as wdn0g,
        nc.sbuf_tensor("wdn1", [128, IC * 128], BF16) as wdn1g,
        nc.sbuf_tensor("p_part", [128, K8 * TW], F32) as ppartg,
        nc.sbuf_tensor("p_xt", [128, K8 * TW], F32) as pxtg,
        nc.sbuf_tensor("p_ffl0", [128, TW], F32) as pffl0g,
        nc.sbuf_tensor("p_ffl1", [128, TW], F32) as pffl1g,
    ):
        o_in = [oi0g[:, :], oi1g[:, :]]
        o_int, OB = ointg[:, :], OBg[:, :]
        wdnb = [wdn0g[:, :], wdn1g[:, :]]
        p_part = ppartg[:, :]
        p_xt = pxtg[:, :]
        p_ffl = [pffl0g[:, :], pffl1g[:, :]]

        pe_dn_marks = []
        ac_dn_marks = []
        ac_ob_all = []
        last_tile_pe = [PEc.n]

        for g in range(G):
            t0 = g * TW
            ac_ob_marks = []
            ltp = last_tile_pe[0]
            for quar in range(4):
                b = quar % 2
                ldm = [0]
                pre_ac = ac_ob_marks[quar - 2] if quar >= 2 else ACc.n

                def o_sync(sync, quar=quar, b=b, ldm=ldm, pre_ac=pre_ac, t0=t0, ltp=ltp):
                    sync.wait_ge(ACc.h, pre_ac)
                    sync.wait_ge(PEc.h, ltp)
                    sync.dma_start(out=o_in[b].rearrange("p (i t) -> p i t", i=6),
                                   in_=s_o[6 * quar:6 * quar + 6, :, t0:t0 + TW].rearrange("i p t -> p i t")).then_inc(LD.h, 16)
                    LD.n += 16
                    ldm[0] = LD.n
                block.sync(o_sync)

                pre_ob = ac_ob_marks[quar - 1] if quar >= 1 else (ac_ob_all[-1] if ac_ob_all else 0)

                def o_dv(v, b=b, ldm=ldm, g=g, quar=quar, pre_ob=pre_ob):
                    v.wait_ge(LD.h, ldm[0])
                    v.wait_ge(ACc.h, pre_ob)
                    if g == 0 and quar == 0:
                        v.wait_ge(DVc.h, dv_s5)
                    v.tensor_scalar(o_int, o_in[b], sclbc[:, 9:10], None, ALU.mult).then_inc(DVc.h, 1)
                    DVc.n += 1
                block.vector(o_dv)
                mk = DVc.n

                def o_ac(a, quar=quar):
                    a.wait_ge(DVc.h, mk)
                    a.activation(OB[:, quar * 6 * TW:(quar + 1) * 6 * TW], o_int, AF.Copy,
                                 scale=sclbc[:, 8:9]).then_inc(ACc.h, 1)
                    ACc.n += 1
                    ac_ob_marks.append(ACc.n)
                    ac_ob_all.append(ACc.n)
                block.scalar(o_ac)

            ldp = [0]
            dvo_prev = DVc.n

            def p_sync(sync, t0=t0, ldp=ldp, dvo_prev=dvo_prev):
                sync.wait_ge(DVc.h, dvo_prev)
                sync.dma_start(out=p_part.rearrange("p (k t) -> p k t", k=K8),
                               in_=s_part[:, :, t0:t0 + TW].rearrange("k p t -> p k t")).then_inc(LD.h, 16)
                LD.n += 16
                sync.dma_start(out=p_xt.rearrange("p (k t) -> p k t", k=K8),
                               in_=xfm.rearrange("(k p) t -> p k t", p=128)[:, :, t0:t0 + TW]).then_inc(LD.h, 16)
                LD.n += 16
                ldp[0] = LD.n
            block.sync(p_sync)

            dv_dn_marks = []
            for h in range(K8):
                b = h % 2
                idx = g * K8 + h
                ldw = [0]

                def w_sync(sync, h=h, b=b, ldw=ldw, idx=idx):
                    if idx >= 2:
                        sync.wait_ge(PEc.h, pe_dn_marks[idx - 2])
                    sync.dma_start(out=wdnb[b].rearrange("p (i c) -> p i c", c=128),
                                   in_=s_wdnT[h, :, :, :].rearrange("i p c -> p i c")).then_inc(LD.h, 16)
                    LD.n += 16
                    ldw[0] = LD.n
                block.sync(w_sync)

                def d_pe(t, h=h, b=b, ldw=ldw, idx=idx):
                    t.wait_ge(LD.h, ldw[0])
                    t.wait_ge(ACc.h, ac_ob_marks[3])
                    if idx >= 2:
                        t.wait_ge(ACc.h, ac_dn_marks[idx - 2])
                    pp = psRot[h % 2]
                    for i in range(IC):
                        mm = t.matmul(pp[:, :], wdnb[b][:, i * 128:(i + 1) * 128],
                                      OB[:, i * TW:(i + 1) * TW], start=(i == 0), stop=(i == IC - 1))
                    mm.then_inc(PEc.h, 1)
                    PEc.n += 1
                    pe_dn_marks.append(PEc.n)
                block.tensor(d_pe)
                pem = PEc.n

                def d_ac(a, h=h, pem=pem):
                    a.wait_ge(PEc.h, pem)
                    if h >= 2:
                        a.wait_ge(DVc.h, dv_dn_marks[h - 2])
                    a.activation(p_ffl[h % 2], psRot[h % 2][:, :], AF.Copy).then_inc(ACc.h, 1)
                    ACc.n += 1
                    ac_dn_marks.append(ACc.n)
                block.scalar(d_ac)
                acm = ACc.n

                def d_dv(v, h=h, ldp=ldp, acm=acm):
                    v.wait_ge(ACc.h, acm)
                    v.wait_ge(LD.h, ldp[0])
                    v.tensor_tensor(p_part[:, h * TW:(h + 1) * TW], p_part[:, h * TW:(h + 1) * TW],
                                    p_ffl[h % 2], ALU.add)
                    v.tensor_tensor(p_part[:, h * TW:(h + 1) * TW], p_part[:, h * TW:(h + 1) * TW],
                                    p_xt[:, h * TW:(h + 1) * TW], ALU.add).then_inc(DVc.h, 1)
                    DVc.n += 1
                    dv_dn_marks.append(DVc.n)
                block.vector(d_dv)
            last_tile_pe[0] = PEc.n
            dv_out = DVc.n

            def out_sync(sync, t0=t0, dv_out=dv_out):
                sync.wait_ge(DVc.h, dv_out)
                sync.dma_start(out=outfm.rearrange("(k p) t -> p k t", p=128)[:, :, t0:t0 + TW],
                               in_=p_part.rearrange("p (k t) -> p k t", k=K8)).then_inc(SO.h, 16)
                SO.n += 16
            block.sync(out_sync)

        fin = SO.n

        def fin_sync(sync):
            sync.wait_ge(SO.h, fin)
        block.sync(fin_sync)

    st_.close()
    return nc


def make_inputs(core, TOK, x_full_fm_shard, w):
    import ml_dtypes
    b_qkv = w["b_qkv"]
    im = {
        "xfm": np.ascontiguousarray(x_full_fm_shard),
        "wqkv": np.ascontiguousarray(w["w_qkv"]),
        "wgate_a": np.ascontiguousarray(w["w_gate_a"]),
        "wproj": np.ascontiguousarray(w["w_proj"]),
        "bqk": np.ascontiguousarray(np.concatenate([
            b_qkv[0:1024].reshape(8, 128).T,
            b_qkv[1024:2048].reshape(8, 128).T,
            w["b_gate"].reshape(8, 128).T], axis=1).astype(np.float32)),
        "bv_row": np.ascontiguousarray(b_qkv[2048:3072].reshape(1, 1024)),
        "bproj_d": np.ascontiguousarray(w["b_proj"].reshape(8, 128).T),
        "lng_d": np.ascontiguousarray(w["ln_g"].reshape(8, 128).T),
        "lnb_d": np.ascontiguousarray(w["ln_b"].reshape(8, 128).T),
        "wup_d": np.ascontiguousarray(w["w_up"]),
        "wgf_d": np.ascontiguousarray(w["w_gate_f"]),
        "wdn_d": np.ascontiguousarray(w["w_down"]),
        "identb_d": np.eye(128, dtype=np.float32).astype(ml_dtypes.bfloat16),
        "identf_d": np.eye(128, dtype=np.float32),
        "onesrow_d": np.ones((1, 128), dtype=np.float32),
        "onescol_d": np.ones((128, 1), dtype=np.float32).astype(ml_dtypes.bfloat16),
        "mask_d": np.ascontiguousarray(
            np.concatenate([np.triu(np.ones((128, 128), dtype=np.float32)),
                            np.ones((128, TW - 128), dtype=np.float32)], axis=1)),
        "flagr_d": np.full((128, 1), 1.0 if core % 2 == 1 else 0.0, dtype=np.float32),
        "flags_d": np.full((128, 1), 0.0 if core % 2 == 1 else 1.0, dtype=np.float32),
    }
    return im


def kernel_run(x, w, TOK, dbg=False):
    from concourse.bass_utils import run_bass_kernel_spmd
    B, T, Dm = x.shape
    n = 8
    flat = np.ascontiguousarray(x.reshape(B * T, Dm))
    assert B * T == n * TOK
    nc = build(TOK, dbg=dbg)
    in_maps = []
    for c in range(n):
        shard = np.ascontiguousarray(flat[c * TOK:(c + 1) * TOK, :].T)
        in_maps.append(make_inputs(c, TOK, shard, w))
    res = run_bass_kernel_spmd(nc, in_maps, core_ids=list(range(n)))
    outs = [np.ascontiguousarray(r["outfm"].T) for r in res.results]
    full = np.concatenate(outs, axis=0).reshape(B, T, Dm)
    return full, res

def _stage_weights(L0_attn_w_qkv, L0_attn_b_qkv, L0_attn_w_gate, L0_attn_b_gate,
                   L0_attn_w_proj, L0_attn_b_proj, L0_attn_ln_g, L0_attn_ln_b,
                   L0_ffn_w_gate, L0_ffn_w_up, L0_ffn_w_down):
    return {
        "w_qkv": np.ascontiguousarray(L0_attn_w_qkv, dtype=np.float32),
        "b_qkv": np.ascontiguousarray(L0_attn_b_qkv, dtype=np.float32),
        "w_gate_a": np.ascontiguousarray(L0_attn_w_gate, dtype=np.float32),
        "b_gate": np.ascontiguousarray(L0_attn_b_gate, dtype=np.float32),
        "w_proj": np.ascontiguousarray(L0_attn_w_proj, dtype=np.float32),
        "b_proj": np.ascontiguousarray(L0_attn_b_proj, dtype=np.float32),
        "ln_g": np.ascontiguousarray(L0_attn_ln_g, dtype=np.float32),
        "ln_b": np.ascontiguousarray(L0_attn_ln_b, dtype=np.float32),
        "w_up": np.ascontiguousarray(L0_ffn_w_up, dtype=np.float32),
        "w_gate_f": np.ascontiguousarray(L0_ffn_w_gate, dtype=np.float32),
        "w_down": np.ascontiguousarray(L0_ffn_w_down, dtype=np.float32),
    }


def kernel(L0_attn_x, L0_attn_w_qkv, L0_attn_b_qkv, L0_attn_w_gate, L0_attn_b_gate,
           L0_attn_w_proj, L0_attn_b_proj, L0_attn_ln_g, L0_attn_ln_b,
           L0_ffn_w_gate, L0_ffn_w_up, L0_ffn_w_down):
    from concourse.bass_utils import run_bass_kernel_spmd
    x = np.ascontiguousarray(L0_attn_x, dtype=np.float32)
    B, T, Dm = x.shape
    n = 8
    TOKf = (B * T) // n
    w = _stage_weights(L0_attn_w_qkv, L0_attn_b_qkv, L0_attn_w_gate, L0_attn_b_gate,
                       L0_attn_w_proj, L0_attn_b_proj, L0_attn_ln_g, L0_attn_ln_b,
                       L0_ffn_w_gate, L0_ffn_w_up, L0_ffn_w_down)
    nc = build(TOKf)
    flat = x.reshape(B * T, Dm)
    in_maps = []
    for c in range(n):
        shard = np.ascontiguousarray(flat[c * TOKf:(c + 1) * TOKf, :].T)
        in_maps.append(make_inputs(c, TOKf, shard, w))
    res = run_bass_kernel_spmd(nc, in_maps, core_ids=list(range(n)))
    outs = [np.ascontiguousarray(r["outfm"].T) for r in res.results]
    return np.concatenate(outs, axis=0).reshape(B, T, Dm)


# version 08 (final)
# speedup vs baseline: 1.1248x; 1.1248x over previous
import contextlib
import numpy as np
import concourse.bass as bass
import concourse.mybir as mybir

F32 = mybir.dt.float32
BF16 = mybir.dt.bfloat16
I32 = mybir.dt.int32
ALU = mybir.AluOpType
AF = mybir.ActivationFunctionType
AX = mybir.AxisListType.X

D = 1024
K8 = D // 128
I_FFN = 3072
IC = I_FFN // 128
TW = 512
LN_EPS = 1e-5
DEN_EPS = 1e-6
R32767 = float(np.float32(1.0) / np.float32(32767.0))
R1024 = 1.0 / 1024.0
C65535 = 65535.0 / 65536.0
R65536 = 1.0 / 65536.0
R7 = float(np.float32(1.0) / np.float32(7.0))


class Sem:
    def __init__(self, stack, nc, name):
        self.h = stack.enter_context(nc.semaphore(name))
        self.n = 0

    def inc(self, instr, k=1):
        instr.then_inc(self.h, k)
        self.n += k
        return self.n


def build(TOK, dbg=False, stop_after=None, p1a_cut=99):
    G = TOK // TW
    nc = bass.Bass("TRN2", target_bir_lowering=False, detect_race_conditions=False)
    st_ = contextlib.ExitStack()

    def din(name, shape, dt=F32):
        return nc.dram_tensor(name, shape, dt, kind="ExternalInput").ap()

    xfm = din("xfm", [D, TOK])
    wqkv = din("wqkv", [D, 3 * D])
    wgate_a = din("wgate_a", [D, D])
    wproj = din("wproj", [D, D])
    bqk = din("bqk", [128, 24])
    bv_row = din("bv_row", [1, D])
    bproj_d = din("bproj_d", [128, 8])
    lng_d = din("lng_d", [128, 8])
    lnb_d = din("lnb_d", [128, 8])
    wup_d = din("wup_d", [I_FFN, D])
    wgf_d = din("wgf_d", [I_FFN, D])
    wdn_d = din("wdn_d", [D, I_FFN])
    identb_d = din("identb_d", [128, 128], BF16)
    identf_d = din("identf_d", [128, 128])
    onesrow_d = din("onesrow_d", [1, 128])
    onescol_d = din("onescol_d", [128, 1], BF16)
    mask_d = din("mask_d", [128, TW])
    flagr_d = din("flagr_d", [128, 1])
    flags_d = din("flags_d", [128, 1])

    outfm = nc.dram_tensor("outfm", [D, TOK], F32, kind="ExternalOutput").ap()
    dbg_out = {}
    if dbg:
        for nm, shape, dt in [
            ("d_xn", [K8, 128, TW], F32), ("d_xn2", [K8, 128, TW], BF16),
            ("d_kfm", [K8, 128, TW], BF16), ("d_vtm", [4, 128, D], BF16),
            ("d_S", [128, K8 * D], F32), ("d_ksum", [128, 8], F32),
            ("d_scl", [128, 16], F32), ("d_gt0", [128, TOK], F32),
            ("d_up0", [128, TOK], F32), ("d_g20", [128, 1024], F32),
            ("d_o0", [128, 1024], F32), ("d_part", [K8, 128, TW], F32),
            ("d_q", [K8, 128, TW], BF16), ("d_y", [K8, 128, TW], BF16),
            ("d_den", [1, TW], F32), ("d_wup0", [K8, 128, 128], BF16),
            ("d_xq", [K8, 128, 1024], BF16), ("d_W1", [128, 8192], BF16),
        ]:
            dbg_out[nm] = nc.dram_tensor(nm, shape, dt, kind="ExternalOutput").ap()

    s_xn = nc.dram_tensor("s_xn", [K8, 128, TOK], F32).ap()
    s_xn2 = nc.dram_tensor("s_xn2", [K8, 128, TOK], BF16).ap()
    s_qfm = nc.dram_tensor("s_qfm", [K8, 128, TOK], BF16).ap()
    s_kfm = nc.dram_tensor("s_kfm", [K8, 128, TOK], BF16).ap()
    s_ktm = nc.dram_tensor("s_ktm", [TOK // 128, 128, D], BF16).ap()
    s_vtm = nc.dram_tensor("s_vtm", [TOK // 128, 128, D], BF16).ap()
    s_up = nc.dram_tensor("s_up", [IC, 128, TOK], F32).ap()
    s_gt = nc.dram_tensor("s_gt", [IC, 128, TOK], F32).ap()
    s_g2 = nc.dram_tensor("s_g2", [IC, 128, TOK], F32).ap()
    s_o = nc.dram_tensor("s_o", [IC, 128, TOK], F32).ap()
    s_part = nc.dram_tensor("s_part", [K8, 128, TOK], F32).ap()
    s_wupT = nc.dram_tensor("s_wupT", [IC, K8, 128, 128], BF16).ap()
    s_wgtT = nc.dram_tensor("s_wgtT", [IC, K8, 128, 128], BF16).ap()
    s_wdnT = nc.dram_tensor("s_wdnT", [K8, IC, 128, 128], BF16).ap()
    cc_i, cc_o = {}, {}
    for nm in ["m1", "m23", "m4", "m5"]:
        cc_i[nm] = nc.dram_tensor(f"cc_{nm}_i", [128, 1], F32).ap()
        cc_o[nm] = nc.dram_tensor(f"cc_{nm}_o", [128, 1], F32).ap()
    cc_si = nc.dram_tensor("cc_si", [128, K8 * D + 8], F32).ap()
    cc_so = nc.dram_tensor("cc_so", [128, K8 * D + 8], F32).ap()

    def sb(name, shape, dt):
        return nc.alloc_sbuf_tensor(name, shape, dt).ap()

    identb = sb("identb", [128, 128], BF16)
    identf = sb("identf", [128, 128], F32)
    onesrow = sb("onesrow", [1, 128], F32)
    onescol = sb("onescol", [128, 1], BF16)
    maskt = sb("maskt", [128, TW], F32)
    flagr = sb("flagr", [128, 1], F32)
    flags = sb("flags", [128, 1], F32)
    bqk_s = sb("bqk_s", [128, 24], F32)
    bproj_s = sb("bproj_s", [128, 8], F32)
    lng_s = sb("lng_s", [128, 8], F32)
    lnb_s = sb("lnb_s", [128, 8], F32)
    bvtile = sb("bvtile", [128, D], F32)
    sclbc = sb("sclbc", [128, 16], F32)
    rows = sb("rows", [1, 3616], F32)
    ksum = sb("ksum", [128, 8], F32)
    ksum_b = sb("ksum_b", [128, 8], BF16)
    m1acc = sb("m1acc", [128, 1], F32)
    m2acc = sb("m2acc", [128, 1], F32)
    m3acc = sb("m3acc", [128, 1], F32)
    m4acc = sb("m4acc", [128, 1], F32)
    m5acc = sb("m5acc", [128, 1], F32)
    dmy0 = sb("dmy0", [128, 64], F32)
    dmy1 = sb("dmy1", [128, 64], F32)
    mg128 = sb("mg128", [128, 1], F32)
    mr128 = sb("mr128", [128, 1], F32)
    S = sb("S", [128, K8 * D], F32)
    Sb = sb("Sb", [128, K8 * D], BF16)

    def ps(name, shape, dt=F32):
        return nc.alloc_psum_tensor(name, shape, dt).ap()

    psA = ps("psA", [128, TW])
    psB = ps("psB", [128, TW])
    psC = ps("psC", [128, TW])
    psD = ps("psD", [128, TW])
    psS1 = ps("psS1", [128, 512])
    psS2 = ps("psS2", [128, 512])
    psT = ps("psT", [128, 2048], BF16)
    psRot = [psA, psB, psC, psD]

    LD = Sem(st_, nc, "LD")
    LDG = Sem(st_, nc, "LDG")
    SO = Sem(st_, nc, "SO")
    PEc = Sem(st_, nc, "PEc")
    DVc = Sem(st_, nc, "DVc")
    ACc = Sem(st_, nc, "ACc")
    GPc = Sem(st_, nc, "GPc")
    CCc = Sem(st_, nc, "CCc")
    BARc = Sem(st_, nc, "BARc")

    block = st_.enter_context(nc.Block())

    def VSP(v, n=2):
        for _ in range(n):
            v.tensor_copy(dmy1[:, :], dmy0[:, :])

    def _finish():
        fin0 = SO.n

        def fs(sync):
            sync.wait_ge(SO.h, fin0)
            sync.wait_ge(LD.h, LD.n)
        block.sync(fs)
        st_.close()
    bar_total = [0]

    def barrier():
        waits = [(PEc.h, PEc.n), (DVc.h, DVc.n), (ACc.h, ACc.n),
                 (LD.h, LD.n), (SO.h, SO.n), (GPc.h, GPc.n), (LDG.h, LDG.n)]
        for sec in (block.sync, block.tensor, block.vector, block.scalar, block.gpsimd):
            def f(eng, waits=waits):
                for h, n in waits:
                    if n > 0:
                        eng.wait_ge(h, n)
            sec(f)

    last_mrow_gp = [0]
    last_mrec_dv = [0]

    def finish_max_send(accs, cc_name):
        assert len(accs) == 1
        acc = accs[0]
        pe_mark = [0]

        def f_pe(t):
            t.wait_ge(DVc.h, DVc.n)
            t.transpose(psS1[0:1, 0:128], acc[:, :], identf[:, :]).then_inc(PEc.h, 1)
            PEc.n += 1
            pe_mark[0] = PEc.n
        block.tensor(f_pe)

        dvm = [0]

        def f_dv(v):
            v.wait_ge(PEc.h, pe_mark[0])
            v.tensor_reduce(rows[0:1, 3600:3601], psS1[0:1, 0:128], AX, ALU.max).then_inc(DVc.h, 1)
            DVc.n += 1
            dvm[0] = DVc.n
        block.vector(f_dv)

        pe2 = [0]

        def f_pe2(t):
            t.wait_ge(DVc.h, dvm[0])
            t.matmul(psS2[:, 0:2], onesrow[:, :], rows[0:1, 3600:3602],
                     start=True, stop=True).then_inc(PEc.h, 1)
            PEc.n += 1
            pe2[0] = PEc.n
        block.tensor(f_pe2)

        dvm2 = [0]

        def f_dv2(v):
            v.wait_ge(PEc.h, pe2[0])
            v.wait_ge(GPc.h, last_mrow_gp[0])
            v.tensor_copy(mg128[:, 0:1], psS2[:, 0:1]).then_inc(DVc.h, 1)
            DVc.n += 1
            dvm2[0] = DVc.n
        block.vector(f_dv2)

        def f_gp(gp):
            gp.wait_ge(CCc.h, CCc.n)
            gp.wait_ge(DVc.h, dvm2[0])
            gp.dma_start(out=cc_i[cc_name], in_=mg128[:, 0:1]).then_inc(GPc.h, 16)
            GPc.n += 16
            last_mrow_gp[0] = GPc.n
            gp.wait_ge(GPc.h, GPc.n)
            gp.collective_compute("AllReduce", ALU.max, replica_groups=[list(range(8))],
                                  ins=[cc_i[cc_name]], outs=[cc_o[cc_name]]).then_inc(CCc.h, 1)
            CCc.n += 1
        block.gpsimd(f_gp)
        return CCc.n

    def recv_scales(cc_name, cc_val, n_scl, scl_cols, qmax_recips):
        assert n_scl == 1
        gpm = [0]

        def r_gp(gp):
            gp.wait_ge(CCc.h, cc_val)
            gp.wait_ge(DVc.h, last_mrec_dv[0])
            gp.dma_start(out=mr128[:, 0:1], in_=cc_o[cc_name]).then_inc(GPc.h, 16)
            GPc.n += 16
            gpm[0] = GPc.n
        block.gpsimd(r_gp)

        def r_dv(v):
            v.wait_ge(GPc.h, gpm[0])
            v.tensor_scalar(sclbc[:, scl_cols[0]:scl_cols[0] + 1], mr128[:, 0:1],
                            qmax_recips[0], 1e-12, ALU.mult, ALU.max)
            VSP(v)
            v.reciprocal(sclbc[:, scl_cols[0] + 1:scl_cols[0] + 2],
                         sclbc[:, scl_cols[0]:scl_cols[0] + 1]).then_inc(DVc.h, 1)
            DVc.n += 1
            last_mrec_dv[0] = DVc.n
        block.vector(r_dv)
        return DVc.n

    with (
        nc.sbuf_tensor("t_wf", [128, 3072], F32) as t_wfg,
        nc.sbuf_tensor("t_dq1", [128, 3072], F32) as t1g,
        nc.sbuf_tensor("t_dqi", [128, 3072], I32) as tig,
        nc.sbuf_tensor("t_dqs", [128, 96], F32) as tsg,
        nc.sbuf_tensor("t_dqr", [128, 96], F32) as trg,
        nc.sbuf_tensor("t_dqb", [128, 3072], BF16) as tbg,
        nc.sbuf_tensor("t_stage", [128, 3072], BF16) as stgg,
        nc.sbuf_tensor("bvrow", [1, D], F32) as bvrowg,
    ):
        bvrow = bvrowg[:, :]
        t_wf, t1, ti, ts_, tr_, tb, stg = (t_wfg[:, :], t1g[:, :], tig[:, :], tsg[:, :],
                                           trg[:, :], tbg[:, :], stgg[:, :])

        def s0_sync(sync):
            for dst, src in [(identb, identb_d), (identf, identf_d),
                             (onesrow, onesrow_d), (onescol, onescol_d),
                             (maskt, mask_d), (flagr, flagr_d), (flags, flags_d),
                             (bqk_s, bqk), (bproj_s, bproj_d), (lng_s, lng_d),
                             (lnb_s, lnb_d), (bvrow, bv_row)]:
                sync.dma_start(out=dst[:, :], in_=src).then_inc(LD.h, 16)
                LD.n += 16
        block.sync(s0_sync)
        ld_consts = LD.n

        def s0_dv(v):
            v.memset(mg128[:, :], 0.0)
            v.memset(dmy0[:, :], 0.0)
            v.memset(dmy1[:, :], 0.0)
            v.memset(mr128[:, :], 0.0)
            v.memset(sclbc[:, :], 0.0)
            v.memset(rows[:, :], 0.0)
            v.memset(m1acc[:, :], 0.0)
            v.memset(m2acc[:, :], 0.0)
            v.memset(m3acc[:, :], 0.0)
            v.memset(m4acc[:, :], 0.0)
            v.memset(m5acc[:, :], 0.0)
            v.memset(ksum[:, :], 0.0)
            v.memset(S[:, :], 0.0).then_inc(DVc.h, 1)
            DVc.n += 1
        block.vector(s0_dv)

        def s0_pe(t):
            t.wait_ge(LD.h, ld_consts)
            t.matmul(psS1[:, :], onesrow[:, :], bvrow[0:1, 0:512], start=True, stop=True)
            t.matmul(psS2[:, :], onesrow[:, :], bvrow[0:1, 512:1024], start=True, stop=True).then_inc(PEc.h, 1)
            PEc.n += 1
        block.tensor(s0_pe)
        pe_bv = PEc.n

        def s0_dv2(v):
            v.wait_ge(PEc.h, pe_bv)
            v.tensor_copy(bvtile[:, 0:512], psS1[:, :])
            v.tensor_copy(bvtile[:, 512:1024], psS2[:, :]).then_inc(DVc.h, 1)
            DVc.n += 1
        block.vector(s0_dv2)

        def dequant3(src_ap, n_rowchunks, rowlen, store_fn):
            nblk = rowlen // 32
            nhc = rowlen // 128
            for oc in range(n_rowchunks):
                ldm = [0]
                so_pre = SO.n
                dv_pre = DVc.n

                def q_sync(sync, oc=oc, ldm=ldm, so_pre=so_pre, dv_pre=dv_pre):
                    sync.wait_ge(SO.h, so_pre)
                    sync.wait_ge(DVc.h, dv_pre)
                    sync.dma_start(out=t_wf[:, 0:rowlen], in_=src_ap[oc * 128:(oc + 1) * 128, :]).then_inc(LD.h, 16)
                    LD.n += 16
                    ldm[0] = LD.n
                block.sync(q_sync)

                def q_dv(v, ldm=ldm):
                    v.wait_ge(LD.h, ldm[0])
                    w3 = t_wf[:, 0:rowlen].rearrange("p (b c) -> p b c", c=32)
                    v.tensor_reduce(ts_[:, 0:nblk], w3, AX, ALU.max, apply_absolute_value=True)
                    VSP(v)
                    v.tensor_scalar(ts_[:, 0:nblk], ts_[:, 0:nblk], R7, 1e-12, ALU.mult, ALU.max)
                    VSP(v)
                    v.reciprocal(tr_[:, 0:nblk], ts_[:, 0:nblk])
                    VSP(v)
                    rb = tr_[:, 0:nblk].unsqueeze(2).broadcast_to([128, nblk, 32])
                    v.tensor_tensor(t1[:, 0:rowlen].rearrange("p (b c) -> p b c", c=32), w3, rb, ALU.mult)
                    v.tensor_scalar(ti[:, 0:rowlen], t1[:, 0:rowlen], 0.0, None, ALU.add).then_inc(DVc.h, 1)
                    DVc.n += 1
                block.vector(q_dv)
                dvq = DVc.n

                def q_ac(a):
                    a.wait_ge(DVc.h, dvq)
                    a.activation(t1[:, 0:rowlen], ti[:, 0:rowlen], AF.Copy).then_inc(ACc.h, 1)
                    ACc.n += 1
                block.scalar(q_ac)
                acq = ACc.n

                def q_dv2(v):
                    v.wait_ge(ACc.h, acq)
                    sbb = ts_[:, 0:nblk].unsqueeze(2).broadcast_to([128, nblk, 32])
                    v.tensor_tensor(tb[:, 0:rowlen].rearrange("p (b c) -> p b c", c=32),
                                    t1[:, 0:rowlen].rearrange("p (b c) -> p b c", c=32), sbb, ALU.mult).then_inc(DVc.h, 1)
                    DVc.n += 1
                block.vector(q_dv2)
                dvq2 = DVc.n
                pe_marks = []
                dv_tr_base = DVc.n

                def q_pe(t):
                    t.wait_ge(DVc.h, dvq2)
                    for h in range(nhc):
                        slot = h % 2
                        if h >= 2:
                            t.wait_ge(DVc.h, dv_tr_base + h - 1)
                        t.transpose(psT[:, slot * 1024:slot * 1024 + 128], tb[:, h * 128:(h + 1) * 128], identb[:, :]).then_inc(PEc.h, 1)
                        PEc.n += 1
                        pe_marks.append(PEc.n)
                block.tensor(q_pe)

                def q_dv3(v):
                    for h in range(nhc):
                        v.wait_ge(PEc.h, pe_marks[h])
                        v.tensor_copy(stg[:, h * 128:(h + 1) * 128], psT[:, (h % 2) * 1024:(h % 2) * 1024 + 128]).then_inc(DVc.h, 1)
                        DVc.n += 1
                block.vector(q_dv3)
                dvq3 = DVc.n

                def q_sync2(sync, oc=oc):
                    sync.wait_ge(DVc.h, dvq3)
                    store_fn(sync, oc)
                block.sync(q_sync2)

        def store_up(scratch):
            def f(sync, oc):
                sync.dma_start(out=scratch[oc, :, :, :].rearrange("h p c -> p h c"),
                               in_=stg[:, 0:1024].rearrange("p (h c) -> p h c", c=128)).then_inc(SO.h, 16)
                SO.n += 16
            return f

        def store_dn(sync, hc):
            sync.dma_start(out=s_wdnT[hc, :, :, :].rearrange("i p c -> p i c"),
                           in_=stg[:, 0:3072].rearrange("p (i c) -> p i c", c=128)).then_inc(SO.h, 16)
            SO.n += 16

        dequant3(wup_d, IC, 1024, store_up(s_wupT))
        dequant3(wgf_d, IC, 1024, store_up(s_wgtT))
        dequant3(wdn_d, K8, 3072, store_dn)

    if dbg:
        def s0_dbg(sync):
            sync.wait_ge(SO.h, SO.n)
            sync.dma_start(out=dbg_out["d_wup0"], in_=s_wupT[0, :, :, :]).then_inc(SO.h, 16)
            SO.n += 16
        block.sync(s0_dbg)
    barrier()
    if stop_after == "s0":
        _finish()
        return nc

    with (
        nc.sbuf_tensor("xb0", [128, K8 * TW], F32) as xb0g,
        nc.sbuf_tensor("xb16", [128, K8 * TW], BF16) as xb16g,
        nc.sbuf_tensor("xsq16", [128, K8 * TW], BF16) as xsq16g,
        nc.sbuf_tensor("xnb", [128, K8 * TW], F32) as xnbg,
        nc.sbuf_tensor("xnw", [128, K8 * TW], F32) as xnwg,
        nc.sbuf_tensor("xn2b", [128, K8 * TW], BF16) as xn2bg,
        nc.sbuf_tensor("mtmp", [128, 1], F32) as mtmpg,
    ):
        xb = [xb0g[:, :], xb0g[:, :]]
        xb16, xsq16, xnb, xnw, xn2b, mtmp = (xb16g[:, :], xsq16g[:, :], xnbg[:, :],
                                             xnwg[:, :], xn2bg[:, :], mtmpg[:, :])
        dv_free = [DVc.n, DVc.n]
        ac_free = [ACc.n, ACc.n]
        prev_pe3 = [PEc.n]
        prev_dv7 = [DVc.n]
        prev_ac5 = [ACc.n]
        prev_so = [SO.n]

        def newton_rsqrt(v):
            ve = rows[0:1, 1536:2048]
            s_ = rows[0:1, 1024:1536]
            tmp = rows[0:1, 3072:3584]
            v.reciprocal(rows[0:1, 2560:3072], s_)
            return v.tensor_copy(rows[0:1, 2048:2560], rows[0:1, 0:512])

        for g in range(G):
            p = g % 2
            t0 = g * TW
            ldm = [0]
            dv_tile_pre = DVc.n
            dvp, acp = max(dv_free), max(ac_free)

            def p1_sync(sync, p=p, t0=t0, ldm=ldm, dvp=dvp, acp=acp):
                sync.wait_ge(DVc.h, dvp)
                sync.wait_ge(ACc.h, acp)
                sync.dma_start(
                    out=xb[p].rearrange("p (k t) -> p k t", k=K8),
                    in_=xfm.rearrange("(k p) t -> p k t", p=128)[:, :, t0:t0 + TW],
                ).then_inc(LD.h, 16)
                LD.n += 16
                ldm[0] = LD.n
            block.sync(p1_sync)

            pp3 = prev_pe3[0]

            def p1_dv1(v, p=p, ldm=ldm, pp3=pp3):
                v.wait_ge(LD.h, ldm[0])
                v.wait_ge(PEc.h, pp3)
                v.tensor_copy(xb16, xb[p]).then_inc(DVc.h, 1)
                DVc.n += 1
            block.vector(p1_dv1)
            dv1 = DVc.n

            def p1_ac1(a, p=p, ldm=ldm, pp3=pp3):
                a.wait_ge(LD.h, ldm[0])
                a.wait_ge(PEc.h, pp3)
                a.activation(xsq16, xb[p], AF.Square).then_inc(ACc.h, 1)
                ACc.n += 1
            block.scalar(p1_ac1)
            ac1 = ACc.n
            pd7 = dv_tile_pre

            def p1_pe1(t, pd7=pd7):
                t.wait_ge(DVc.h, dv1)
                t.wait_ge(ACc.h, ac1)
                t.wait_ge(DVc.h, pd7)
                for k in range(K8):
                    t.matmul(psS1[0:1, :], onescol[:, :], xb16[:, k * TW:(k + 1) * TW],
                             start=(k == 0), stop=(k == K8 - 1))
                for k in range(K8):
                    mm = t.matmul(psS2[0:1, :], onescol[:, :], xsq16[:, k * TW:(k + 1) * TW],
                                  start=(k == 0), stop=(k == K8 - 1))
                mm.then_inc(PEc.h, 1)
                PEc.n += 1
            block.tensor(p1_pe1)
            pe1 = PEc.n
            prev_pe3[0] = pe1

            if p1a_cut < 2:
                continue

            def p1_dv2(v):
                v.wait_ge(PEc.h, pe1)
                v.tensor_scalar(rows[0:1, 0:512], psS1[0:1, :], R1024, None, ALU.mult)
                v.tensor_scalar(rows[0:1, 512:1024], psS2[0:1, :], R1024, None, ALU.mult)
                VSP(v)
                v.scalar_tensor_tensor(rows[0:1, 1024:1536], rows[0:1, 0:512], -1.0, rows[0:1, 0:512], ALU.mult, ALU.mult)
                VSP(v)
                v.scalar_tensor_tensor(rows[0:1, 1536:2048], rows[0:1, 512:1024], LN_EPS, rows[0:1, 1024:1536], ALU.add, ALU.add).then_inc(DVc.h, 1)
                DVc.n += 1
            block.vector(p1_dv2)
            dv2 = DVc.n

            if p1a_cut < 3:
                continue

            def p1_ac2(a):
                a.wait_ge(DVc.h, dv2)
                a.activation(rows[0:1, 1024:1536], rows[0:1, 1536:2048], AF.Sqrt).then_inc(ACc.h, 1)
                ACc.n += 1
            block.scalar(p1_ac2)
            ac2 = ACc.n

            def p1_dv3(v):
                v.wait_ge(ACc.h, ac2)
                newton_rsqrt(v).then_inc(DVc.h, 1)
                DVc.n += 1
            block.vector(p1_dv3)
            dv3 = DVc.n

            def p1_pe2(t):
                t.wait_ge(DVc.h, dv3)
                t.matmul(psS1[:, :], onesrow[:, :], rows[0:1, 2048:2560], start=True, stop=True)
                t.matmul(psS2[:, :], onesrow[:, :], rows[0:1, 2560:3072], start=True, stop=True).then_inc(PEc.h, 1)
                PEc.n += 1
            block.tensor(p1_pe2)
            pe2 = PEc.n
            pa5 = prev_ac5[0]
            pso = prev_so[0]
            if p1a_cut < 4:
                continue

            def p1_dv4(v, p=p, pa5=pa5, pso=pso):
                v.wait_ge(PEc.h, pe2)
                v.wait_ge(ACc.h, pa5)
                v.wait_ge(SO.h, pso)
                for k in range(K8):
                    v.tensor_tensor(xnw[:, k * TW:(k + 1) * TW], xb[p][:, k * TW:(k + 1) * TW], psS1[:, :], ALU.subtract)
                for k in range(K8):
                    v.tensor_tensor(xnb[:, k * TW:(k + 1) * TW], xnw[:, k * TW:(k + 1) * TW], psS2[:, :], ALU.mult)
                v.tensor_reduce(mtmp, xnb, AX, ALU.max, apply_absolute_value=True)
                VSP(v)
                v.tensor_tensor(m1acc[:, :], m1acc[:, :], mtmp, ALU.max)
                v.tensor_copy(xb16, xnb).then_inc(DVc.h, 1)
                DVc.n += 1
            block.vector(p1_dv4)
            dv4 = DVc.n

            if p1a_cut < 5:
                dv_free[p] = dv4
                continue

            def p1_ac3(a):
                a.wait_ge(DVc.h, dv4)
                a.activation(xsq16, xnb, AF.Square).then_inc(ACc.h, 1)
                ACc.n += 1
            block.scalar(p1_ac3)
            ac3 = ACc.n

            def p1_pe3(t):
                t.wait_ge(ACc.h, ac3)
                for k in range(K8):
                    t.matmul(psS1[0:1, :], onescol[:, :], xb16[:, k * TW:(k + 1) * TW],
                             start=(k == 0), stop=(k == K8 - 1))
                for k in range(K8):
                    mm = t.matmul(psS2[0:1, :], onescol[:, :], xsq16[:, k * TW:(k + 1) * TW],
                                  start=(k == 0), stop=(k == K8 - 1))
                mm.then_inc(PEc.h, 1)
                PEc.n += 1
            block.tensor(p1_pe3)
            pe3 = PEc.n
            prev_pe3[0] = pe3

            def p1_dv5(v):
                v.wait_ge(PEc.h, pe3)
                v.tensor_scalar(rows[0:1, 0:512], psS1[0:1, :], R1024, None, ALU.mult)
                v.tensor_scalar(rows[0:1, 512:1024], psS2[0:1, :], R1024, None, ALU.mult)
                VSP(v)
                v.scalar_tensor_tensor(rows[0:1, 1024:1536], rows[0:1, 0:512], -1.0, rows[0:1, 0:512], ALU.mult, ALU.mult)
                VSP(v)
                v.scalar_tensor_tensor(rows[0:1, 1536:2048], rows[0:1, 512:1024], LN_EPS, rows[0:1, 1024:1536], ALU.add, ALU.add).then_inc(DVc.h, 1)
                DVc.n += 1
            block.vector(p1_dv5)
            dv5 = DVc.n

            if p1a_cut < 6:
                dv_free[p] = dv4
                continue

            def p1_ac4(a):
                a.wait_ge(DVc.h, dv5)
                a.activation(rows[0:1, 1024:1536], rows[0:1, 1536:2048], AF.Sqrt).then_inc(ACc.h, 1)
                ACc.n += 1
            block.scalar(p1_ac4)
            ac4 = ACc.n

            def p1_dv6(v):
                v.wait_ge(ACc.h, ac4)
                newton_rsqrt(v).then_inc(DVc.h, 1)
                DVc.n += 1
            block.vector(p1_dv6)
            dv6 = DVc.n

            def p1_pe4(t):
                t.wait_ge(DVc.h, dv6)
                t.matmul(psS1[:, :], onesrow[:, :], rows[0:1, 2048:2560], start=True, stop=True)
                t.matmul(psS2[:, :], onesrow[:, :], rows[0:1, 2560:3072], start=True, stop=True).then_inc(PEc.h, 1)
                PEc.n += 1
            block.tensor(p1_pe4)
            pe4 = PEc.n

            if p1a_cut < 7:
                dv_free[p] = dv4
                continue

            def p1_dv7(v):
                v.wait_ge(PEc.h, pe4)
                for k in range(K8):
                    v.tensor_tensor(xnw[:, k * TW:(k + 1) * TW], xnb[:, k * TW:(k + 1) * TW], psS1[:, :], ALU.subtract)
                last = None
                for k in range(K8):
                    last = v.tensor_tensor(xnw[:, k * TW:(k + 1) * TW], xnw[:, k * TW:(k + 1) * TW], psS2[:, :], ALU.mult)
                last.then_inc(DVc.h, 1)
                DVc.n += 1
            block.vector(p1_dv7)
            dv7 = DVc.n
            prev_dv7[0] = dv7
            dv_free[p] = dv4

            def p1_ac5(a):
                a.wait_ge(DVc.h, dv7)
                last = None
                for k in range(K8):
                    last = a.activation(xn2b[:, k * TW:(k + 1) * TW], xnw[:, k * TW:(k + 1) * TW], AF.Identity,
                                        bias=lnb_s[:, k:k + 1], scale=lng_s[:, k:k + 1])
                last.then_inc(ACc.h, 1)
                ACc.n += 1
            block.scalar(p1_ac5)
            ac5 = ACc.n
            prev_ac5[0] = ac5
            ac_free[p] = ac1

            def p1_sync2(sync, g=g, t0=t0):
                sync.wait_ge(DVc.h, dv4)
                sync.dma_start(out=s_xn[:, :, t0:t0 + TW].rearrange("k p t -> p k t"),
                               in_=xnb.rearrange("p (k t) -> p k t", k=K8)).then_inc(SO.h, 16)
                SO.n += 16
                sync.wait_ge(ACc.h, ac5)
                sync.dma_start(out=s_xn2[:, :, t0:t0 + TW].rearrange("k p t -> p k t"),
                               in_=xn2b.rearrange("p (k t) -> p k t", k=K8)).then_inc(SO.h, 16)
                SO.n += 16
                if dbg and g == 0:
                    sync.dma_start(out=dbg_out["d_xn"].rearrange("k p t -> p k t"), in_=xnb.rearrange("p (k t) -> p k t", k=K8)).then_inc(SO.h, 16)
                    SO.n += 16
                    sync.dma_start(out=dbg_out["d_xn2"].rearrange("k p t -> p k t"), in_=xn2b.rearrange("p (k t) -> p k t", k=K8)).then_inc(SO.h, 16)
                    SO.n += 16
            block.sync(p1_sync2)
            prev_so[0] = SO.n

        if stop_after == "p1a_tiles":
            _finish()
            return nc
        cc_m1 = finish_max_send([m1acc], "m1")
        dv_after_m1 = DVc.n
    p1a_done_so = SO.n
    if stop_after == "p1a":
        _finish()
        return nc

    with (
        nc.sbuf_tensor("a_xn2", [128, K8 * TW], BF16) as a_xn2g,
        nc.sbuf_tensor("a_kfm", [128, K8 * TW], BF16) as a_kfmg,
        nc.sbuf_tensor("a_ktm", [128, 4 * D], BF16) as a_ktmg,
        nc.sbuf_tensor("a_vtm", [128, 4 * D], BF16) as a_vtmg,
        nc.sbuf_tensor("a_gat", [128, 2 * TW], F32) as a_gatg,
        nc.sbuf_tensor("a_kb", [128, TW], F32) as a_kbg,
        nc.sbuf_tensor("a_kg", [128, TW], F32) as a_kgg,
        nc.sbuf_tensor("a_km", [128, TW], F32) as a_kmg,
        nc.sbuf_tensor("a_ke", [128, TW], F32) as a_keg,
        nc.sbuf_tensor("a_red", [128, 8], F32) as a_redg,
        nc.sbuf_tensor("a_qfm", [128, K8 * TW], BF16) as a_qfmg,
        nc.sbuf_tensor("a_qt", [128, TW], F32) as a_qtg,
        nc.sbuf_tensor("a_qm", [128, TW], F32) as a_qmg,
        nc.sbuf_tensor("a_qe", [128, TW], F32) as a_qeg,
        nc.sbuf_tensor("W1a", [128, 16384], BF16) as W1ag,
        nc.sbuf_tensor("W2a", [128, 8192], BF16) as W2ag,
        nc.sbuf_tensor("WQa", [128, 8192], BF16) as WQag,
        nc.sbuf_tensor("pa_wf", [128, 3072], F32) as pa_wfg,
    ):
        a_xn2, a_kfm, a_ktm, a_vtm = a_xn2g[:, :], a_kfmg[:, :], a_ktmg[:, :], a_vtmg[:, :]
        a_gat, a_kb, a_kg, a_km, a_ke, a_red = (a_gatg[:, :], a_kbg[:, :], a_kgg[:, :],
                                                a_kmg[:, :], a_keg[:, :], a_redg[:, :])
        a_qfm, a_qt, a_qm, a_qe = a_qfmg[:, :], a_qtg[:, :], a_qmg[:, :], a_qeg[:, :]

        W1 = W1ag[:, :]
        W2 = W2ag[:, :]
        WQ = WQag[:, :]
        pa_wf = pa_wfg[:, :]
        for k in range(K8):
            ldw = [0]
            acp = ACc.n

            def s0w_sync(sync, k=k, ldw=ldw, acp=acp):
                sync.wait_ge(ACc.h, acp)
                sync.dma_start(out=pa_wf, in_=wqkv[k * 128:(k + 1) * 128, :]).then_inc(LD.h, 16)
                LD.n += 16
                ldw[0] = LD.n
            block.sync(s0w_sync)

            def s0w_ac(a, k=k, ldw=ldw):
                a.wait_ge(LD.h, ldw[0])
                a.activation(WQ[:, k * 1024:(k + 1) * 1024], pa_wf[:, 0:1024], AF.Copy)
                a.activation(W1[:, k * 2048:k * 2048 + 1024], pa_wf[:, 1024:2048], AF.Copy)
                a.activation(W1[:, k * 2048 + 1024:k * 2048 + 2048], pa_wf[:, 2048:3072], AF.Copy).then_inc(ACc.h, 1)
                ACc.n += 1
            block.scalar(s0w_ac)
        for k in range(K8):
            ldw = [0]
            acp = ACc.n

            def s0g_sync(sync, k=k, ldw=ldw, acp=acp):
                sync.wait_ge(ACc.h, acp)
                sync.dma_start(out=pa_wf[:, 0:1024], in_=wgate_a[k * 128:(k + 1) * 128, :]).then_inc(LD.h, 16)
                LD.n += 16
                ldw[0] = LD.n
            block.sync(s0g_sync)

            def s0g_ac(a, k=k, ldw=ldw):
                a.wait_ge(LD.h, ldw[0])
                a.activation(W2[:, k * 1024:(k + 1) * 1024], pa_wf[:, 0:1024], AF.Copy).then_inc(ACc.h, 1)
                ACc.n += 1
            block.scalar(s0g_ac)
        ac_wcast = ACc.n

        for g in range(G):
            t0 = g * TW
            ldm = [0]
            dv_pre, ac_pre, pe_pre, so_pre = DVc.n, ACc.n, PEc.n, SO.n

            def pa_sync(sync, t0=t0, ldm=ldm, dv_pre=dv_pre, pe_pre=pe_pre, so_pre=so_pre):
                sync.wait_ge(DVc.h, dv_pre)
                sync.wait_ge(PEc.h, pe_pre)
                sync.wait_ge(SO.h, so_pre)
                sync.dma_start(out=a_xn2.rearrange("p (k t) -> p k t", k=K8),
                               in_=s_xn2[:, :, t0:t0 + TW].rearrange("k p t -> p k t")).then_inc(LD.h, 16)
                LD.n += 16
                ldm[0] = LD.n
            block.sync(pa_sync)

            ac_gate_base = ACc.n
            dv_kb_base = DVc.n
            pe_gate, pe_khat = [], []

            def pa_pe1(t, ldm=ldm, ac_gate_base=ac_gate_base, dv_kb_base=dv_kb_base):
                t.wait_ge(LD.h, ldm[0])
                t.wait_ge(ACc.h, max(ac_wcast, 0))
                for n in range(K8):
                    if n >= 1:
                        t.wait_ge(ACc.h, ac_gate_base + 2 * n - 1)
                    for k in range(K8):
                        mm = t.matmul(psA[:, :], W2[:, k * 1024 + n * 128:k * 1024 + (n + 1) * 128],
                                      a_xn2[:, k * TW:(k + 1) * TW], start=(k == 0), stop=(k == K8 - 1))
                    mm.then_inc(PEc.h, 1)
                    PEc.n += 1
                    pe_gate.append(PEc.n)
                    if n >= 1:
                        t.wait_ge(DVc.h, dv_kb_base + 2 * n - 1)
                    for k in range(K8):
                        mm = t.matmul(psB[:, :], W1[:, k * 2048 + n * 128:k * 2048 + (n + 1) * 128],
                                      a_xn2[:, k * TW:(k + 1) * TW], start=(k == 0), stop=(k == K8 - 1))
                    mm.then_inc(PEc.h, 1)
                    PEc.n += 1
                    pe_khat.append(PEc.n)
            block.tensor(pa_pe1)

            ac_chain_base = ACc.n

            def pa_ac1(a, ac_chain_base=ac_chain_base, dv_kb_base=dv_kb_base):
                for n in range(K8):
                    a.wait_ge(PEc.h, pe_gate[n])
                    if n >= 2:
                        a.wait_ge(DVc.h, dv_kb_base + 2 * n - 3)
                    a.activation(a_gat[:, (n % 2) * TW:(n % 2 + 1) * TW], psA[:, :], AF.Sigmoid,
                                 bias=bqk_s[:, 16 + n:17 + n]).then_inc(ACc.h, 1)
                    ACc.n += 1
                    a.wait_ge(DVc.h, dv_kb_base + 2 * n + 1)
                    a.activation(a_ke, a_km, AF.Exp).then_inc(ACc.h, 1)
                    ACc.n += 1
            block.scalar(pa_ac1)

            def pa_dv1(v, ac_chain_base=ac_chain_base):
                for n in range(K8):
                    v.wait_ge(PEc.h, pe_khat[n])
                    v.wait_ge(ACc.h, ac_chain_base + 2 * n + 1)
                    v.tensor_scalar(a_kb, psB[:, :], bqk_s[:, 8 + n:9 + n], None, ALU.add)
                    v.tensor_tensor(a_kg, a_kb, a_gat[:, (n % 2) * TW:(n % 2 + 1) * TW], ALU.mult)
                    v.tensor_scalar(a_km, a_kg, 0.0, None, ALU.min).then_inc(DVc.h, 1)
                    DVc.n += 1
                    v.wait_ge(ACc.h, ac_chain_base + 2 * n + 2)
                    v.scalar_tensor_tensor(a_kfm[:, n * TW:(n + 1) * TW], a_kg, 0.0, a_ke, ALU.max, ALU.add).then_inc(DVc.h, 1)
                    DVc.n += 1
            block.vector(pa_dv1)
            dv_kdone = DVc.n

            dv_qe_base = DVc.n
            ac_qexp_base = ACc.n
            pe_qm = []

            def pa_peQ(t, g=g, dv_qe_base=dv_qe_base):
                if g == 0:
                    t.wait_ge(DVc.h, dv_after_m1)
                for n in range(K8):
                    pp = psS1 if n % 2 == 0 else psS2
                    if n >= 2:
                        t.wait_ge(DVc.h, dv_qe_base + (n - 1) * 2)
                    for k in range(K8):
                        mm = t.matmul(pp[:, :], WQ[:, k * 1024 + n * 128:k * 1024 + (n + 1) * 128],
                                      a_xn2[:, k * TW:(k + 1) * TW], start=(k == 0), stop=(k == K8 - 1))
                    mm.then_inc(PEc.h, 1)
                    PEc.n += 1
                    pe_qm.append(PEc.n)
            block.tensor(pa_peQ)

            def pa_dvQ(v, ac_qexp_base=ac_qexp_base):
                for n in range(K8):
                    pp = psS1 if n % 2 == 0 else psS2
                    v.wait_ge(PEc.h, pe_qm[n])
                    v.tensor_scalar(a_qt, pp[:, :], bqk_s[:, n:n + 1], None, ALU.add)
                    v.tensor_scalar(a_qm, a_qt, 0.0, None, ALU.min).then_inc(DVc.h, 1)
                    DVc.n += 1
                    v.wait_ge(ACc.h, ac_qexp_base + n + 1)
                    v.scalar_tensor_tensor(a_qfm[:, n * TW:(n + 1) * TW], a_qt, 0.0, a_qe, ALU.max, ALU.add).then_inc(DVc.h, 1)
                    DVc.n += 1
            block.vector(pa_dvQ)

            def pa_acQ(a, dv_qe_base=dv_qe_base):
                for n in range(K8):
                    a.wait_ge(DVc.h, dv_qe_base + n * 2 + 1)
                    a.activation(a_qe, a_qm, AF.Exp).then_inc(ACc.h, 1)
                    ACc.n += 1
            block.scalar(pa_acQ)
            dv_qdone = DVc.n

            dv_v_base = DVc.n
            pe_v = []

            def pa_pe2(t, dv_v_base=dv_v_base):
                for m in range(8):
                    j, h2 = m // 2, m % 2
                    pp = psC if (m % 2 == 0) else psD
                    if m >= 2:
                        t.wait_ge(DVc.h, dv_v_base + m - 1)
                    for k in range(K8):
                        mm = t.matmul(pp[:, :], a_xn2[:, k * TW + j * 128:k * TW + (j + 1) * 128],
                                      W1[:, k * 2048 + 1024 + h2 * 512:k * 2048 + 1024 + (h2 + 1) * 512],
                                      start=(k == 0), stop=(k == K8 - 1))
                    mm.then_inc(PEc.h, 1)
                    PEc.n += 1
                    pe_v.append(PEc.n)
            block.tensor(pa_pe2)

            def pa_dv2(v):
                for m in range(8):
                    j, h2 = m // 2, m % 2
                    pp = psC if (m % 2 == 0) else psD
                    v.wait_ge(PEc.h, pe_v[m])
                    v.tensor_tensor(a_vtm[:, j * D + h2 * 512:j * D + (h2 + 1) * 512],
                                    pp[:, :], bvtile[:, h2 * 512:(h2 + 1) * 512], ALU.add).then_inc(DVc.h, 1)
                    DVc.n += 1
            block.vector(pa_dv2)
            dv_vdone = DVc.n
            dv_tr_base = DVc.n
            pe_tr = []

            def pa_pe3(t, dv_tr_base=dv_tr_base):
                t.wait_ge(DVc.h, dv_kdone)
                idx = 0
                for n in range(K8):
                    for j in range(4):
                        slot = idx % 2
                        if idx >= 2:
                            t.wait_ge(DVc.h, dv_tr_base + idx - 1)
                        t.transpose(psT[:, slot * 1024:slot * 1024 + 128],
                                    a_kfm[:, n * TW + j * 128:n * TW + (j + 1) * 128], identb[:, :]).then_inc(PEc.h, 1)
                        PEc.n += 1
                        pe_tr.append(PEc.n)
                        idx += 1
            block.tensor(pa_pe3)

            def pa_dv3(v):
                idx = 0
                for n in range(K8):
                    for j in range(4):
                        v.wait_ge(PEc.h, pe_tr[idx])
                        v.tensor_copy(a_ktm[:, j * D + n * 128:j * D + (n + 1) * 128],
                                      psT[:, (idx % 2) * 1024:(idx % 2) * 1024 + 128]).then_inc(DVc.h, 1)
                        DVc.n += 1
                        idx += 1
            block.vector(pa_dv3)
            dv_trdone = DVc.n

            dv_ds_base = DVc.n
            pe_ds = []

            def pa_pe4(t, dv_ds_base=dv_ds_base):
                t.wait_ge(DVc.h, dv_trdone)
                for m in range(16):
                    d, h2 = m // 2, m % 2
                    pp = psC if (m % 2 == 0) else psD
                    if m >= 2:
                        t.wait_ge(DVc.h, dv_ds_base + m - 1)
                    for j in range(4):
                        mm = t.matmul(pp[:, :], a_ktm[:, j * D + d * 128:j * D + (d + 1) * 128],
                                      a_vtm[:, j * D + h2 * 512:j * D + (h2 + 1) * 512],
                                      start=(j == 0), stop=(j == 3))
                    mm.then_inc(PEc.h, 1)
                    PEc.n += 1
                    pe_ds.append(PEc.n)
            block.tensor(pa_pe4)

            def pa_dv4(v):
                for m in range(16):
                    d, h2 = m // 2, m % 2
                    pp = psC if (m % 2 == 0) else psD
                    v.wait_ge(PEc.h, pe_ds[m])
                    sl = S[:, d * D + h2 * 512:d * D + (h2 + 1) * 512]
                    v.tensor_tensor(sl, sl, pp[:, :], ALU.add).then_inc(DVc.h, 1)
                    DVc.n += 1
                for k in range(K8):
                    v.tensor_reduce(a_red[:, k:k + 1], a_kfm[:, k * TW:(k + 1) * TW], AX, ALU.add)
                VSP(v)
                v.tensor_tensor(ksum[:, 0:8], ksum[:, 0:8], a_red[:, 0:8], ALU.add).then_inc(DVc.h, 1)
                DVc.n += 1
            block.vector(pa_dv4)

            def pa_sync2(sync, g=g, t0=t0):
                sync.wait_ge(DVc.h, dv_kdone)
                sync.dma_start(out=s_kfm[:, :, t0:t0 + TW].rearrange("k p t -> p k t"),
                               in_=a_kfm.rearrange("p (k t) -> p k t", k=K8)).then_inc(SO.h, 16)
                SO.n += 16
                sync.wait_ge(DVc.h, dv_qdone)
                sync.dma_start(out=s_qfm[:, :, t0:t0 + TW].rearrange("k p t -> p k t"),
                               in_=a_qfm.rearrange("p (k t) -> p k t", k=K8)).then_inc(SO.h, 16)
                SO.n += 16
                sync.wait_ge(DVc.h, dv_trdone)
                sync.dma_start(out=s_ktm[4 * g:4 * g + 4, :, :].rearrange("j p d -> p j d"),
                               in_=a_ktm.rearrange("p (j d) -> p j d", j=4)).then_inc(SO.h, 16)
                SO.n += 16
                sync.wait_ge(DVc.h, dv_vdone)
                sync.dma_start(out=s_vtm[4 * g:4 * g + 4, :, :].rearrange("j p d -> p j d"),
                               in_=a_vtm.rearrange("p (j d) -> p j d", j=4)).then_inc(SO.h, 16)
                SO.n += 16
                if dbg and g == 0:
                    sync.dma_start(out=dbg_out["d_kfm"].rearrange("k p t -> p k t"), in_=a_kfm.rearrange("p (k t) -> p k t", k=K8)).then_inc(SO.h, 16)
                    SO.n += 16
                    sync.dma_start(out=dbg_out["d_vtm"].rearrange("j p d -> p j d"), in_=a_vtm.rearrange("p (j d) -> p j d", j=4)).then_inc(SO.h, 16)
                    SO.n += 16
            block.sync(pa_sync2)


        def pa_dvS(v):
            v.tensor_scalar(S[:, :], S[:, :], flags[:, 0:1], None, ALU.mult)
            v.tensor_scalar(ksum[:, :], ksum[:, :], flags[:, 0:1], None, ALU.mult).then_inc(DVc.h, 1)
            DVc.n += 1
        block.vector(pa_dvS)
        dv_smask = DVc.n
        if dbg:
            def pa_dbgS(sync):
                sync.wait_ge(DVc.h, dv_smask)
                sync.dma_start(out=dbg_out["d_S"], in_=S[:, :]).then_inc(SO.h, 16)
                SO.n += 16
                sync.dma_start(out=dbg_out["d_ksum"], in_=ksum[:, :]).then_inc(SO.h, 16)
                SO.n += 16
            block.sync(pa_dbgS)

        def pa_gp(gp):
            gp.wait_ge(CCc.h, CCc.n)
            gp.wait_ge(DVc.h, dv_smask)
            gp.dma_start(out=cc_si[:, 0:K8 * D], in_=S[:, :]).then_inc(GPc.h, 16)
            GPc.n += 16
            gp.dma_start(out=cc_si[:, K8 * D:K8 * D + 8], in_=ksum[:, :]).then_inc(GPc.h, 16)
            GPc.n += 16
            gp.wait_ge(GPc.h, GPc.n)
            gp.collective_compute(
                "AllReduce", ALU.add,
                replica_groups=[[0, 1], [2, 3], [4, 5], [6, 7]],
                ins=[cc_si], outs=[cc_so],
            ).then_inc(CCc.h, 1)
            CCc.n += 1
        block.gpsimd(pa_gp)
        cc_state = CCc.n

    dv_s1 = recv_scales("m1", cc_m1, 1, [0], [R32767])
    barrier()
    if stop_after == "passa":
        _finish()
        return nc

    p3_dv_marks = []
    p3_so_marks = []
    p3_ac_marks = []
    _es2 = contextlib.ExitStack()
    XQg = _es2.enter_context(nc.sbuf_tensor("XQ", [128, K8 * TOK], BF16))
    f_ing = _es2.enter_context(nc.sbuf_tensor("f_in", [128, K8 * TW], F32))
    f_row0g = _es2.enter_context(nc.sbuf_tensor("f_row0", [128, TOK], F32))
    f_wu0g = _es2.enter_context(nc.sbuf_tensor("f_wu0", [128, 1024], BF16))
    f_wu1g = _es2.enter_context(nc.sbuf_tensor("f_wu1", [128, 1024], BF16))
    f_redg = _es2.enter_context(nc.sbuf_tensor("f_red", [128, 1], F32))
    p3i0g = _es2.enter_context(nc.sbuf_tensor("p3_in0", [128, 1024], F32))
    p3i1g = _es2.enter_context(nc.sbuf_tensor("p3_in1", [128, 1024], F32))
    p3ig = _es2.enter_context(nc.sbuf_tensor("p3_int", [128, 1024], I32))
    p3ig2 = _es2.enter_context(nc.sbuf_tensor("p3_int2", [128, 1024], I32))
    p3sfg = _es2.enter_context(nc.sbuf_tensor("p3_sf", [128, 1024], F32))
    p3sfg2 = _es2.enter_context(nc.sbuf_tensor("p3_sf2", [128, 1024], F32))
    p3redg = _es2.enter_context(nc.sbuf_tensor("p3_red", [128, 8], F32))
    if True:
        XQ, f_in = XQg[:, :], f_ing[:, :]
        f_row = [f_row0g[:, :], f_row0g[:, :]]
        if TOK >= K8 * TW:
            f_int = f_row0g[:, :].bitcast(I32)[:, 0:K8 * TW]
        else:
            f_int = _es2.enter_context(nc.sbuf_tensor("f_int", [128, K8 * TW], I32))[:, :]
        f_wu = [f_wu0g[:, :], f_wu1g[:, :]]
        f_red = f_redg[:, :]
        p3_in = [p3i0g[:, :], p3i1g[:, :]]
        p3_int = [p3ig[:, :], p3ig2[:, :]]
        p3_gf = [p3i0g[:, :], p3i1g[:, :]]
        p3_sf = [p3sfg[:, :], p3sfg2[:, :]]
        p3_o = [p3ig[:, :].bitcast(F32), p3ig2[:, :].bitcast(F32)]
        p3_red = p3redg[:, :]

        for g in range(G):
            t0 = g * TW
            ldm = [0]
            acp = ACc.n

            def xq_sync(sync, t0=t0, ldm=ldm, acp=acp):
                sync.wait_ge(ACc.h, acp)
                sync.dma_start(out=f_in.rearrange("p (k t) -> p k t", k=K8),
                               in_=s_xn[:, :, t0:t0 + TW].rearrange("k p t -> p k t")).then_inc(LD.h, 16)
                LD.n += 16
                ldm[0] = LD.n
            block.sync(xq_sync)

            def xq_dv(v, ldm=ldm, g=g):
                v.wait_ge(LD.h, ldm[0])
                if g == 0:
                    v.wait_ge(DVc.h, dv_s1)
                v.tensor_scalar(f_int, f_in, sclbc[:, 1:2], None, ALU.mult).then_inc(DVc.h, 1)
                DVc.n += 1
            block.vector(xq_dv)
            dvx = DVc.n

            def xq_ac(a, g=g, t0=t0):
                a.wait_ge(DVc.h, dvx)
                last = None
                for k in range(K8):
                    last = a.activation(XQ[:, k * TOK + t0:k * TOK + t0 + TW],
                                        f_int[:, k * TW:(k + 1) * TW], AF.Copy,
                                        scale=sclbc[:, 0:1])
                last.then_inc(ACc.h, 1)
                ACc.n += 1
            block.scalar(xq_ac)
            if dbg and g == min(G - 1, 1):
                acxq0 = ACc.n

                def xq_dbg(sync, acxq0=acxq0):
                    sync.wait_ge(ACc.h, acxq0)
                    sync.dma_start(out=dbg_out["d_xq"].rearrange("k p t -> p k t"),
                                   in_=XQ.rearrange("p (k t) -> p k t", t=TOK)[:, :, 0:1024]).then_inc(SO.h, 16)
                    SO.n += 16
                block.sync(xq_dbg)
        ac_xq = ACc.n

        def ffn_mm_phase(w_scratch, s_dst, macc, dbg_key, interleave=None):
            pe_gmarks, pe_rowmarks, so_marks, ac_row_marks, f_dv_marks = [], [], [], [], []
            pe_prev, so_prev, dv_prev = PEc.n, SO.n, DVc.n
            for i in range(IC):
                b = i % 2
                ldm = [0]

                def f_sync(sync, i=i, b=b, ldm=ldm):
                    if i >= 2:
                        sync.wait_ge(PEc.h, pe_rowmarks[i - 2])
                        sync.wait_ge(SO.h, so_marks[i - 2])
                    else:
                        sync.wait_ge(PEc.h, pe_prev)
                        sync.wait_ge(SO.h, so_prev)
                        sync.wait_ge(DVc.h, dv_prev)
                    sync.dma_start(out=f_wu[b].rearrange("p (h c) -> p h c", c=128),
                                   in_=w_scratch[i, :, :, :].rearrange("h p c -> p h c")).then_inc(LDG.h, 16)
                    LDG.n += 16
                    ldm[0] = LDG.n
                block.gpsimd(f_sync)

                ac_base_i = ACc.n

                def f_pe(t, i=i, b=b, ldm=ldm, ac_base_i=ac_base_i):
                    t.wait_ge(LDG.h, ldm[0])
                    if i == 0:
                        t.wait_ge(ACc.h, ac_xq)
                    for g in range(G):
                        pp = psRot[g % 2]
                        if g < 2:
                            if i > 0:
                                t.wait_ge(ACc.h, ac_base_i)
                        else:
                            t.wait_ge(ACc.h, ac_base_i + g - 1)
                        for h in range(K8):
                            mm = t.matmul(pp[:, :], f_wu[b][:, h * 128:(h + 1) * 128],
                                          XQ[:, h * TOK + g * TW:h * TOK + (g + 1) * TW],
                                          start=(h == 0), stop=(h == K8 - 1))
                        mm.then_inc(PEc.h, 1)
                        PEc.n += 1
                        pe_gmarks.append(PEc.n)
                    pe_rowmarks.append(PEc.n)
                block.tensor(f_pe)

                def f_ac(a, i=i, b=b):
                    for g in range(G):
                        pp = psRot[g % 2]
                        if g == 0 and i >= 1:
                            a.wait_ge(SO.h, so_marks[i - 1])
                            a.wait_ge(DVc.h, f_dv_marks[i - 1])
                        if g == 0 and i == 0:
                            a.wait_ge(SO.h, so_prev)
                            a.wait_ge(DVc.h, dv_prev)
                        a.wait_ge(PEc.h, pe_gmarks[i * G + g])
                        a.activation(f_row[b][:, g * TW:(g + 1) * TW], pp[:, :], AF.Copy).then_inc(ACc.h, 1)
                        ACc.n += 1
                block.scalar(f_ac)
                ac_row_marks.append(ACc.n)
                ac_row = ACc.n

                def f_dv(v, b=b):
                    v.wait_ge(ACc.h, ac_row)
                    v.tensor_reduce(f_red, f_row[b], AX, ALU.max, apply_absolute_value=True)
                    VSP(v)
                    v.tensor_tensor(macc[:, :], macc[:, :], f_red, ALU.max).then_inc(DVc.h, 1)
                    DVc.n += 1
                    f_dv_marks.append(DVc.n)
                block.vector(f_dv)

                def f_sync2(sync, i=i, b=b):
                    sync.wait_ge(ACc.h, ac_row)
                    sync.dma_start(out=s_dst[i, :, :], in_=f_row[b]).then_inc(SO.h, 16)
                    SO.n += 16
                    so_marks.append(SO.n)
                    if dbg and i == 0 and dbg_key:
                        sync.dma_start(out=dbg_out[dbg_key], in_=f_row[b]).then_inc(SO.h, 16)
                        SO.n += 16
                block.sync(f_sync2)
                if interleave is not None and i >= 1:
                    interleave()

        ffn_mm_phase(s_wgtT, s_gt, m3acc, "d_gt0")
        gt_stores = SO.n
        cc_m3 = finish_max_send([m3acc], "m23")
        dv_s3 = recv_scales("m23", cc_m3, 1, [4], [R32767])

        NQ = TOK // 1024
        NP3 = IC * NQ
        p3_idx = [0]

        def emit_p3_pairs(npairs):
            for _ in range(npairs):
                i0 = p3_idx[0]
                if i0 >= NP3:
                    return
                pair = [i0] + ([i0 + 1] if i0 + 1 < NP3 else [])
                ldm = [0]

                def p3_sync(sync, pair=pair, i0=i0, ldm=ldm):
                    for u, idx in enumerate(pair):
                        if idx >= 2:
                            sync.wait_ge(DVc.h, p3_dv_marks[idx - 2])
                        if idx >= 4:
                            sync.wait_ge(SO.h, p3_so_marks[idx - 4])
                        c, q = idx // NQ, idx % NQ
                        sync.dma_start(out=p3_in[u][:, :], in_=s_gt[c, :, q * 1024:(q + 1) * 1024]).then_inc(LDG.h, 16)
                        LDG.n += 16
                    ldm[0] = LDG.n
                block.gpsimd(p3_sync)

                def p3_dv1(v, pair=pair, ldm=ldm, i0=i0):
                    v.wait_ge(LDG.h, ldm[0])
                    if i0 == 0:
                        v.wait_ge(DVc.h, dv_s3)
                    for u, idx in enumerate(pair):
                        if idx >= 2:
                            v.wait_ge(ACc.h, p3_ac_marks[idx - 2])
                        v.tensor_scalar(p3_int[u], p3_in[u], sclbc[:, 5:6], None, ALU.mult)
                    v.nop2 = None
                    v.tensor_copy(dmy1[:, :], dmy0[:, :]).then_inc(DVc.h, 1)
                    DVc.n += 1
                block.vector(p3_dv1)
                mk1 = DVc.n

                def p3_ac1(a, pair=pair):
                    a.wait_ge(DVc.h, mk1)
                    last = None
                    for u in range(len(pair)):
                        a.activation(p3_gf[u], p3_int[u], AF.Copy, scale=sclbc[:, 4:5])
                        last = a.activation(p3_sf[u], p3_gf[u], AF.Sigmoid)
                    last.then_inc(ACc.h, 1)
                    ACc.n += 1
                block.scalar(p3_ac1)
                mk2 = ACc.n

                def p3_dv2(v, pair=pair):
                    v.wait_ge(ACc.h, mk2)
                    last = None
                    for u in range(len(pair)):
                        last = v.tensor_scalar(p3_int[u], p3_sf[u], 65536.0, None, ALU.mult)
                    last.then_inc(DVc.h, 1)
                    DVc.n += 1
                block.vector(p3_dv2)
                mk3 = DVc.n

                def p3_ac2(a, pair=pair):
                    a.wait_ge(DVc.h, mk3)
                    last = None
                    for u in range(len(pair)):
                        last = a.activation(p3_sf[u], p3_int[u], AF.Copy, scale=R65536)
                    last.then_inc(ACc.h, 1)
                    ACc.n += 1
                block.scalar(p3_ac2)
                mk4 = ACc.n
                for idx in pair:
                    p3_ac_marks.append(mk4)

                def p3_dv3(v, pair=pair, i0=i0):
                    v.wait_ge(ACc.h, mk4)
                    for u, idx in enumerate(pair):
                        if idx >= 2:
                            v.wait_ge(SO.h, p3_so_marks[idx - 2])
                        v.scalar_tensor_tensor(p3_o[u], p3_sf[u], C65535, p3_gf[u], ALU.min, ALU.mult)
                    for u in range(len(pair)):
                        v.tensor_reduce(p3_red[:, u:u + 1], p3_o[u], AX, ALU.max, apply_absolute_value=True)
                    VSP(v)
                    v.tensor_tensor(m4acc[:, :], m4acc[:, :], p3_red[:, 0:1], ALU.max)
                    if len(pair) > 1:
                        VSP(v)
                        v.tensor_tensor(m4acc[:, :], m4acc[:, :], p3_red[:, 1:2], ALU.max)
                    v.tensor_copy(dmy1[:, :], dmy0[:, :]).then_inc(DVc.h, 1)
                    DVc.n += 1
                block.vector(p3_dv3)
                for idx in pair:
                    p3_dv_marks.append(DVc.n)

                def p3_sync2(sync, pair=pair):
                    sync.wait_ge(DVc.h, p3_dv_marks[-1])
                    for u, idx in enumerate(pair):
                        c, q = idx // NQ, idx % NQ
                        sync.dma_start(out=s_g2[c, :, q * 1024:(q + 1) * 1024], in_=p3_o[u][:, :]).then_inc(SO.h, 16)
                        SO.n += 16
                        p3_so_marks.append(SO.n)
                    if dbg and pair[0] == 0:
                        sync.dma_start(out=dbg_out["d_g20"], in_=p3_o[0][:, :]).then_inc(SO.h, 16)
                        SO.n += 16
                block.sync(p3_sync2)
                p3_idx[0] += len(pair)

        ffn_mm_phase(s_wupT, s_up, m2acc, "d_up0", interleave=lambda: emit_p3_pairs(2))
        up_stores = SO.n
        cc_m2 = finish_max_send([m2acc], "m23")
        dv_s2 = recv_scales("m23", cc_m2, 1, [2], [R32767])

        emit_p3_pairs(NP3)

        cc_m4 = finish_max_send([m4acc], "m4")
        dv_s4 = recv_scales("m4", cc_m4, 1, [6], [R32767])

    barrier()

    _es2.close()
    if stop_after == "p2":
        _finish()
        return nc
    p4_dv_marks = []
    p4_so_marks = []
    p4_idx = [0]
    _esB = contextlib.ExitStack()
    W1bg = _esB.enter_context(nc.sbuf_tensor("W1b", [128, 8192], BF16))
    b_qg = [_esB.enter_context(nc.sbuf_tensor(f"b_q{p}", [128, K8 * TW], BF16)) for p in range(2)]
    b_kfmg = [_esB.enter_context(nc.sbuf_tensor(f"b_kfm{p}", [128, K8 * TW], BF16)) for p in range(2)]
    b_ktmg = _esB.enter_context(nc.sbuf_tensor("b_ktm", [128, 4 * D], BF16))
    b_vtmg = _esB.enter_context(nc.sbuf_tensor("b_vtm", [128, 4 * D], BF16))
    b_yg = _esB.enter_context(nc.sbuf_tensor("b_y", [128, K8 * TW], BF16))
    b_scg = _esB.enter_context(nc.sbuf_tensor("b_sc", [128, 4 * TW], BF16))
    b_drg = _esB.enter_context(nc.sbuf_tensor("b_dr", [128, TW], F32))
    b_prjg = _esB.enter_context(nc.sbuf_tensor("b_prj", [128, K8 * TW], F32))
    b_redg = _esB.enter_context(nc.sbuf_tensor("b_red", [128, 8], F32))
    p4ag = [_esB.enter_context(nc.sbuf_tensor(f"p4_a{u}", [128, 1024], F32)) for u in range(4)]
    p4bg = [_esB.enter_context(nc.sbuf_tensor(f"p4_b{u}", [128, 1024], F32)) for u in range(4)]
    p4ig_ = [_esB.enter_context(nc.sbuf_tensor(f"p4_i{u}", [128, 1024], I32)) for u in range(2)]
    p4redwg = _esB.enter_context(nc.sbuf_tensor("p4_redw", [128, 128], F32))
    if True:
        W1 = W1bg[:, :]
        b_q = [b_qg[0][:, :], b_qg[1][:, :]]
        b_kfm = [b_kfmg[0][:, :], b_kfmg[1][:, :]]
        b_ktm = b_ktmg[:, :]
        b_vtm = b_vtmg[:, :]
        b_y, b_sc, b_dr = b_yg[:, :], b_scg[:, :], b_drg[:, :]
        b_prj, b_red = b_prjg[:, :], b_redg[:, :]
        p4_a = [t[:, :] for t in p4ag]
        p4_b = [t[:, :] for t in p4bg]
        p4_int = [t[:, :] for t in p4ig_]
        p4_intb = p4_int
        p4_g2 = p4_a
        p4_o = [t[:, :].bitcast(F32) for t in p4ig_]
        p4_redw = p4redwg[:, :]
        psTf = psT.bitcast(F32)
        psE, psF = psTf[:, 0:TW], psTf[:, TW:2 * TW]
        psSC = [psE, psF, psS1, psS2]
        psNum = [psA, psB]
        NQ = max(TOK // 1024, 1)
        NP4 = IC * (TOK // 1024)

        for rnd in range(2):
            ldm = [0]
            dvp = DVc.n

            def wc_sync(sync, rnd=rnd, ldm=ldm, dvp=dvp):
                sync.wait_ge(DVc.h, dvp)
                sync.dma_start(out=b_prj.rearrange("p (k c) -> p k c", k=4),
                               in_=wproj.rearrange("(k p) c -> p k c", p=128)[:, 4 * rnd:4 * rnd + 4, 0:1024]).then_inc(LD.h, 16)
                LD.n += 16
                ldm[0] = LD.n
            block.sync(wc_sync)

            def wc_dv(v, rnd=rnd, ldm=ldm):
                v.wait_ge(LD.h, ldm[0])
                last = None
                for kk in range(4):
                    k = 4 * rnd + kk
                    last = v.tensor_copy(W1[:, k * 1024:(k + 1) * 1024],
                                         b_prj[:, kk * 1024:(kk + 1) * 1024])
                last.then_inc(DVc.h, 1)
                DVc.n += 1
            block.vector(wc_dv)
        dv_wqp = DVc.n
        if dbg:
            def dbg_w1(sync):
                sync.wait_ge(DVc.h, dv_wqp)
                sync.dma_start(out=dbg_out["d_W1"], in_=W1[:, :]).then_inc(SO.h, 16)
                SO.n += 16
            block.sync(dbg_w1)

        lds = [0]

        def bi_syncS(sync):
            sync.wait_ge(CCc.h, cc_state)
            sync.dma_start(out=S[:, :], in_=cc_so[:, 0:K8 * D]).then_inc(LD.h, 16)
            LD.n += 16
            sync.dma_start(out=ksum[:, :], in_=cc_so[:, K8 * D:K8 * D + 8]).then_inc(LD.h, 16)
            LD.n += 16
            lds[0] = LD.n
        block.sync(bi_syncS)

        def bi_dvS(v):
            v.wait_ge(LD.h, lds[0])
            v.tensor_scalar(S[:, :], S[:, :], flagr[:, 0:1], None, ALU.mult)
            v.tensor_scalar(ksum[:, :], ksum[:, :], flagr[:, 0:1], None, ALU.mult).then_inc(DVc.h, 1)
            DVc.n += 1
        block.vector(bi_dvS)
        dv_sinit = DVc.n

        def bi_acS(a):
            a.wait_ge(DVc.h, dv_sinit)
            a.activation(Sb[:, :], S[:, :], AF.Copy)
            a.activation(ksum_b[:, :], ksum[:, :], AF.Copy).then_inc(ACc.h, 1)
            ACc.n += 1
        block.scalar(bi_acS)
        ac_sb = [ACc.n]

        ld_tile = [0] * (G + 2)
        pe_num_last_l = [0] * (G + 2)
        dv_supd_l = [0] * (G + 2)
        dv_drt_l = [0] * (G + 2)
        dv_y_l = [0] * (G + 2)
        so_store_l = [0] * (G + 2)
        ac_pr_l = [0] * (G + 2)

        def emit_loads(gq):
            t0q = gq * TW
            wpe = pe_num_last_l[gq - 2] if gq >= 2 else 0
            wdv = dv_supd_l[gq - 2] if gq >= 2 else 0
            p = gq % 2

            def gl(gp, gq=gq, t0q=t0q, wpe=wpe, wdv=wdv, p=p):
                if wpe:
                    gp.wait_ge(PEc.h, wpe)
                if wdv:
                    gp.wait_ge(DVc.h, wdv)
                gp.dma_start(out=b_q[p].rearrange("p (k t) -> p k t", k=K8),
                             in_=s_qfm[:, :, t0q:t0q + TW].rearrange("k p t -> p k t")).then_inc(LDG.h, 16)
                LDG.n += 16
                gp.dma_start(out=b_kfm[p].rearrange("p (k t) -> p k t", k=K8),
                             in_=s_kfm[:, :, t0q:t0q + TW].rearrange("k p t -> p k t")).then_inc(LDG.h, 16)
                LDG.n += 16
            block.gpsimd(gl)
            ld_tile[gq] = LDG.n

        ld2_tile = [0] * (G + 2)

        def emit_loads2(gq):
            t0q = gq * TW
            wpe = pe_num_last_l[gq - 1] if gq >= 1 else 0

            def gl2(gp, gq=gq, t0q=t0q, wpe=wpe):
                if wpe:
                    gp.wait_ge(PEc.h, wpe)
                gp.dma_start(out=b_ktm.rearrange("p (j d) -> p j d", j=4),
                             in_=s_ktm[4 * gq:4 * gq + 4, :, :].rearrange("j p d -> p j d")).then_inc(LDG.h, 16)
                LDG.n += 16
                gp.dma_start(out=b_vtm.rearrange("p (j d) -> p j d", j=4),
                             in_=s_vtm[4 * gq:4 * gq + 4, :, :].rearrange("j p d -> p j d")).then_inc(LDG.h, 16)
                LDG.n += 16
            block.gpsimd(gl2)
            ld2_tile[gq] = LDG.n

        def emit_p4_chunks(nchunks):
            while nchunks > 0:
                i0 = p4_idx[0]
                if i0 >= NP4:
                    return
                pair = [i0] + ([i0 + 1] if i0 + 1 < NP4 else [])
                nchunks -= len(pair)
                ldm = [0]
                sb = ((i0 // 2) % 2) * 2

                def s_(gp, pair=pair, ldm=ldm, sb=sb):
                    for u, idx in enumerate(pair):
                        if idx >= 4:
                            gp.wait_ge(DVc.h, p4_dv_marks[idx - 4])
                            gp.wait_ge(SO.h, p4_so_marks[idx - 4])
                        c, q = idx // NQ, idx % NQ
                        gp.dma_start(out=p4_a[sb + u][:, :], in_=s_g2[c, :, q * 1024:(q + 1) * 1024]).then_inc(LDG.h, 16)
                        gp.dma_start(out=p4_b[sb + u][:, :], in_=s_up[c, :, q * 1024:(q + 1) * 1024]).then_inc(LDG.h, 16)
                        LDG.n += 32
                    ldm[0] = LDG.n
                block.gpsimd(s_)

                def a1_(a, pair=pair, ldm=ldm, i0=i0, sb=sb):
                    a.wait_ge(LDG.h, ldm[0])
                    if i0 == 0:
                        a.wait_ge(DVc.h, dv_s4)
                    nu = len(pair)

                    def spc():
                        if nu == 1:
                            a.activation(dmy1[:, :], dmy0[:, :], AF.Copy)
                    last = None
                    for u in range(nu):
                        last = a.activation(p4_int[u], p4_a[sb + u], AF.Copy, scale=sclbc[:, 7:8])
                    spc()
                    for u in range(nu):
                        last = a.activation(p4_g2[sb + u], p4_int[u], AF.Copy, scale=sclbc[:, 6:7])
                    spc()
                    for u in range(nu):
                        last = a.activation(p4_intb[u], p4_b[sb + u], AF.Copy, scale=sclbc[:, 3:4])
                    spc()
                    for u in range(nu):
                        last = a.activation(p4_b[sb + u], p4_intb[u], AF.Copy, scale=sclbc[:, 2:3])
                    last.then_inc(ACc.h, 1)
                    ACc.n += 1
                block.scalar(a1_)
                mk4 = ACc.n

                def v3_(v, pair=pair, sb=sb):
                    v.wait_ge(ACc.h, mk4)
                    for u, idx in enumerate(pair):
                        v.tensor_tensor(p4_a[sb + u], p4_g2[sb + u], p4_b[sb + u], ALU.mult)
                    for u, idx in enumerate(pair):
                        v.tensor_reduce(p4_redw[:, idx:idx + 1], p4_a[sb + u], AX, ALU.max,
                                        apply_absolute_value=True).then_inc(DVc.h, 1)
                        DVc.n += 1
                block.vector(v3_)
                for idx in pair:
                    p4_dv_marks.append(DVc.n)

                def s2_(sync, pair=pair, sb=sb):
                    sync.wait_ge(DVc.h, p4_dv_marks[-1])
                    for u, idx in enumerate(pair):
                        c, q = idx // NQ, idx % NQ
                        sync.dma_start(out=s_o[c, :, q * 1024:(q + 1) * 1024], in_=p4_a[sb + u][:, :]).then_inc(SO.h, 16)
                        SO.n += 16
                        p4_so_marks.append(SO.n)
                    if dbg and pair[0] == 0:
                        sync.dma_start(out=dbg_out["d_o0"], in_=p4_a[sb][:, :]).then_inc(SO.h, 16)
                        SO.n += 16
                block.sync(s2_)
                p4_idx[0] += len(pair)

        p4_per_tile = (NP4 + G - 1) // G

        emit_loads(0)
        emit_loads2(0)
        if G > 1:
            emit_loads(1)

        for g in range(G):
            t0 = g * TW
            if g >= 1 and g + 1 < G:
                emit_loads(g + 1)

            pe_sc = []
            w_drt_prev = dv_drt_l[g - 1] if g >= 1 else 0

            def b_pesc(t, g=g, w_drt_prev=w_drt_prev):
                t.wait_ge(LDG.h, ld_tile[g])
                if w_drt_prev:
                    t.wait_ge(DVc.h, w_drt_prev)
                for j in range(4):
                    N = TW - 128 * j
                    pp = psSC[j]
                    for k in range(K8):
                        mm = t.matmul(pp[:, 0:N], b_kfm[g % 2][:, k * TW + j * 128:k * TW + (j + 1) * 128],
                                      b_q[g % 2][:, k * TW + j * 128:k * TW + TW], start=(k == 0), stop=(k == K8 - 1))
                    mm.then_inc(PEc.h, 1)
                    PEc.n += 1
                    pe_sc.append(PEc.n)
            block.tensor(b_pesc)

            def b_dvsc(v):
                for j in range(4):
                    N = TW - 128 * j
                    pp = psSC[j]
                    v.wait_ge(PEc.h, pe_sc[j])
                    v.tensor_tensor(b_sc[:, j * TW:j * TW + N], pp[:, 0:N], maskt[:, 0:N], ALU.mult).then_inc(DVc.h, 1)
                    DVc.n += 1
            block.vector(b_dvsc)
            dv_scdone = DVc.n

            pe_den = [0]

            def b_peden(t, g=g):
                t.wait_ge(DVc.h, dv_scdone)
                t.wait_ge(ACc.h, ac_sb[0])
                for d in range(K8):
                    t.matmul(psS1[0:1, :], ksum_b[:, d:d + 1], b_q[g % 2][:, d * TW:(d + 1) * TW],
                             start=(d == 0), stop=False)
                for j in range(4):
                    N = TW - 128 * j
                    mm = t.matmul(psS1[0:1, 128 * j:TW], onescol[:, :], b_sc[:, j * TW:j * TW + N],
                                  start=False, stop=(j == 3))
                mm.then_inc(PEc.h, 1)
                PEc.n += 1
                pe_den[0] = PEc.n
            block.tensor(b_peden)

            def b_dvden(v):
                v.wait_ge(PEc.h, pe_den[0])
                v.tensor_scalar(rows[0:1, 0:TW], psS1[0:1, :], DEN_EPS, None, ALU.add)
                VSP(v)
                v.reciprocal(rows[0:1, TW:2 * TW], rows[0:1, 0:TW]).then_inc(DVc.h, 1)
                DVc.n += 1
            block.vector(b_dvden)
            dv_den = DVc.n
            if dbg and g == 0:
                def b_dbgden(sync):
                    sync.wait_ge(DVc.h, dv_den)
                    sync.dma_start(out=dbg_out["d_den"], in_=rows[0:1, 0:TW]).then_inc(SO.h, 16)
                    SO.n += 16
                block.sync(b_dbgden)

            pe_denb = [0]

            def b_pednb(t):
                t.wait_ge(DVc.h, dv_den)
                t.matmul(psS2[:, :], onesrow[:, :], rows[0:1, TW:2 * TW], start=True, stop=True).then_inc(PEc.h, 1)
                PEc.n += 1
                pe_denb[0] = PEc.n
            block.tensor(b_pednb)

            def b_dvdnb(v):
                v.wait_ge(PEc.h, pe_denb[0])
                v.tensor_copy(b_dr, psS2[:, :]).then_inc(DVc.h, 1)
                DVc.n += 1
            block.vector(b_dvdnb)
            dv_drt = DVc.n
            dv_drt_l[g] = dv_drt

            dv_ds2_base = DVc.n
            pe_ds2 = []
            w_supd_prev = dv_supd_l[g - 1] if g >= 1 else 0

            def b_peds(t, g=g, w_supd_prev=w_supd_prev, dv_ds2_base=dv_ds2_base):
                t.wait_ge(LDG.h, ld2_tile[g])
                if w_supd_prev:
                    t.wait_ge(DVc.h, w_supd_prev)
                for m in range(16):
                    d, h2 = m // 2, m % 2
                    pp = psC if (m % 2 == 0) else psD
                    if m >= 2:
                        t.wait_ge(DVc.h, dv_ds2_base + m - 1)
                    for j in range(4):
                        mm = t.matmul(pp[:, :], b_ktm[:, j * D + d * 128:j * D + (d + 1) * 128],
                                      b_vtm[:, j * D + h2 * 512:j * D + (h2 + 1) * 512],
                                      start=(j == 0), stop=(j == 3))
                    mm.then_inc(PEc.h, 1)
                    PEc.n += 1
                    pe_ds2.append(PEc.n)
            block.tensor(b_peds)

            def b_dvds(v, g=g):
                v.wait_ge(ACc.h, ac_sb[0])
                for m in range(16):
                    d, h2 = m // 2, m % 2
                    pp = psC if (m % 2 == 0) else psD
                    v.wait_ge(PEc.h, pe_ds2[m])
                    sl = S[:, d * D + h2 * 512:d * D + (h2 + 1) * 512]
                    v.tensor_tensor(sl, sl, pp[:, :], ALU.add).then_inc(DVc.h, 1)
                    DVc.n += 1
                for k in range(K8):
                    v.tensor_reduce(b_red[:, k:k + 1], b_kfm[g % 2][:, k * TW:(k + 1) * TW], AX, ALU.add)
                VSP(v)
                v.tensor_tensor(ksum[:, 0:8], ksum[:, 0:8], b_red[:, 0:8], ALU.add).then_inc(DVc.h, 1)
                DVc.n += 1
            block.vector(b_dvds)
            dv_supd = DVc.n
            dv_supd_l[g] = dv_supd

            dv_num_base = DVc.n
            pe_num = []
            pe_sb_free = [0]
            w_y_prev = dv_y_l[g - 1] if g >= 1 else 0
            w_pr_prev = ac_pr_l[g - 1] if g >= 1 else 0

            def b_penum(t, g=g, dv_num_base=dv_num_base, w_y_prev=w_y_prev, w_pr_prev=w_pr_prev):
                t.wait_ge(DVc.h, dv_scdone)
                t.wait_ge(ACc.h, ac_sb[0])
                if w_y_prev:
                    t.wait_ge(DVc.h, w_y_prev)
                if w_pr_prev:
                    t.wait_ge(ACc.h, w_pr_prev)
                for m in range(K8):
                    pp = psNum[m % 2]
                    if m >= 2:
                        t.wait_ge(DVc.h, dv_num_base + m - 1)
                    for d in range(K8):
                        t.matmul(pp[:, :], Sb[:, d * D + m * 128:d * D + (m + 1) * 128],
                                 b_q[g % 2][:, d * TW:(d + 1) * TW], start=(d == 0), stop=False)
                    for j in range(4):
                        N = TW - 128 * j
                        mm = t.matmul(pp[:, 128 * j:TW], b_vtm[:, j * D + m * 128:j * D + (m + 1) * 128],
                                      b_sc[:, j * TW:j * TW + N], start=False, stop=(j == 3))
                    mm.then_inc(PEc.h, 1)
                    PEc.n += 1
                    pe_num.append(PEc.n)
                pe_sb_free[0] = PEc.n
            block.tensor(b_penum)
            pe_num_last_l[g] = PEc.n

            def b_dvnum(v):
                v.wait_ge(DVc.h, dv_drt)
                for m in range(K8):
                    pp = psNum[m % 2]
                    v.wait_ge(PEc.h, pe_num[m])
                    v.tensor_tensor(b_y[:, m * TW:(m + 1) * TW], pp[:, :], b_dr, ALU.mult).then_inc(DVc.h, 1)
                    DVc.n += 1
            block.vector(b_dvnum)
            dv_ydone = DVc.n
            dv_y_l[g] = dv_ydone

            def b_acSb(a, pe_sb_free=pe_sb_free, dv_supd=dv_supd):
                a.wait_ge(DVc.h, dv_supd)
                a.wait_ge(PEc.h, pe_sb_free[0])
                a.activation(Sb[:, :], S[:, :], AF.Copy)
                a.activation(ksum_b[:, :], ksum[:, :], AF.Copy).then_inc(ACc.h, 1)
                ACc.n += 1
                ac_sb[0] = ACc.n
            block.scalar(b_acSb)

            ac_pr_base = ACc.n
            pe_pr = []

            def b_pepr(t, g=g, ac_pr_base=ac_pr_base, dv_ydone=dv_ydone):
                t.wait_ge(DVc.h, dv_ydone)
                if g == 0:
                    t.wait_ge(DVc.h, dv_wqp)
                for m in range(K8):
                    pp = psNum[m % 2]
                    if m >= 2:
                        t.wait_ge(ACc.h, ac_pr_base + m - 1)
                    for d in range(K8):
                        mm = t.matmul(pp[:, :], W1[:, d * 1024 + m * 128:d * 1024 + (m + 1) * 128],
                                      b_y[:, d * TW:(d + 1) * TW], start=(d == 0), stop=(d == K8 - 1))
                    mm.then_inc(PEc.h, 1)
                    PEc.n += 1
                    pe_pr.append(PEc.n)
            block.tensor(b_pepr)

            w_so_prev = so_store_l[g - 1] if g >= 1 else 0

            def b_acpr(a, w_so_prev=w_so_prev):
                for m in range(K8):
                    pp = psNum[m % 2]
                    a.wait_ge(PEc.h, pe_pr[m])
                    if m < 2 and w_so_prev:
                        a.wait_ge(SO.h, w_so_prev)
                    a.activation(b_prj[:, m * TW:(m + 1) * TW], pp[:, :], AF.Identity,
                                 bias=bproj_s[:, m:m + 1]).then_inc(ACc.h, 1)
                    ACc.n += 1
            block.scalar(b_acpr)
            ac_pr_last = ACc.n
            ac_pr_l[g] = ac_pr_last

            if g + 1 < G:
                emit_loads2(g + 1)

            emit_p4_chunks(p4_per_tile)

            def b_sync2(sync, g=g, t0=t0, ac_pr_last=ac_pr_last):
                sync.wait_ge(ACc.h, ac_pr_last)
                sync.dma_start(out=s_part[:, :, t0:t0 + TW].rearrange("k p t -> p k t"),
                               in_=b_prj.rearrange("p (k t) -> p k t", k=K8)).then_inc(SO.h, 16)
                SO.n += 16
                if dbg and g == 0:
                    sync.dma_start(out=dbg_out["d_part"].rearrange("k p t -> p k t"), in_=b_prj.rearrange("p (k t) -> p k t", k=K8)).then_inc(SO.h, 16)
                    SO.n += 16
                    sync.dma_start(out=dbg_out["d_q"].rearrange("k p t -> p k t"), in_=b_q[0].rearrange("p (k t) -> p k t", k=K8)).then_inc(SO.h, 16)
                    SO.n += 16
                    sync.dma_start(out=dbg_out["d_y"].rearrange("k p t -> p k t"), in_=b_y.rearrange("p (k t) -> p k t", k=K8)).then_inc(SO.h, 16)
                    SO.n += 16
            block.sync(b_sync2)
            so_store_l[g] = SO.n

        emit_p4_chunks(NP4)

        if NP4 > 0:
            def p4_fin(v):
                VSP(v)
                v.tensor_reduce(m5acc[:, :], p4_redw[:, 0:NP4], AX, ALU.max).then_inc(DVc.h, 1)
                DVc.n += 1
            block.vector(p4_fin)
        cc_m5 = finish_max_send([m5acc], "m5")
        dv_s5 = recv_scales("m5", cc_m5, 1, [8], [R32767])
        if dbg:
            def dbg_scl(sync):
                sync.wait_ge(DVc.h, dv_s5)
                sync.dma_start(out=dbg_out["d_scl"], in_=sclbc[:, :]).then_inc(SO.h, 16)
                SO.n += 16
            block.sync(dbg_scl)

    barrier()

    _esB.close()
    if stop_after == "phaseb":
        _finish()
        return nc
    with (
        nc.sbuf_tensor("o_in0", [128, 6 * TW], F32) as oi0g,
        nc.sbuf_tensor("o_in1", [128, 6 * TW], F32) as oi1g,
        nc.sbuf_tensor("o_int", [128, 6 * TW], I32) as ointg,
        nc.sbuf_tensor("OB", [128, IC * TW], BF16) as OBg,
        nc.sbuf_tensor("wdn0", [128, IC * 128], BF16) as wdn0g,
        nc.sbuf_tensor("wdn1", [128, IC * 128], BF16) as wdn1g,
        nc.sbuf_tensor("p_part", [128, K8 * TW], F32) as ppartg,
        nc.sbuf_tensor("p_xt", [128, K8 * TW], F32) as pxtg,
        nc.sbuf_tensor("p_ffl0", [128, TW], F32) as pffl0g,
        nc.sbuf_tensor("p_ffl1", [128, TW], F32) as pffl1g,
    ):
        o_in = [oi0g[:, :], oi1g[:, :]]
        o_int, OB = ointg[:, :], OBg[:, :]
        wdnb = [wdn0g[:, :], wdn1g[:, :]]
        p_part = ppartg[:, :]
        p_xt = pxtg[:, :]
        p_ffl = [pffl0g[:, :], pffl1g[:, :]]

        pe_dn_marks = []
        ac_dn_marks = []
        ac_ob_all = []
        last_tile_pe = [PEc.n]

        for g in range(G):
            t0 = g * TW
            ac_ob_marks = []
            ltp = last_tile_pe[0]
            for quar in range(4):
                b = quar % 2
                ldm = [0]
                pre_ac = ac_ob_marks[quar - 2] if quar >= 2 else ACc.n

                def o_sync(sync, quar=quar, b=b, ldm=ldm, pre_ac=pre_ac, t0=t0, ltp=ltp):
                    sync.wait_ge(ACc.h, pre_ac)
                    sync.wait_ge(PEc.h, ltp)
                    sync.dma_start(out=o_in[b].rearrange("p (i t) -> p i t", i=6),
                                   in_=s_o[6 * quar:6 * quar + 6, :, t0:t0 + TW].rearrange("i p t -> p i t")).then_inc(LDG.h, 16)
                    LDG.n += 16
                    ldm[0] = LDG.n
                block.gpsimd(o_sync)

                pre_ob = ac_ob_marks[quar - 1] if quar >= 1 else (ac_ob_all[-1] if ac_ob_all else 0)

                def o_dv(v, b=b, ldm=ldm, g=g, quar=quar, pre_ob=pre_ob):
                    v.wait_ge(LDG.h, ldm[0])
                    v.wait_ge(ACc.h, pre_ob)
                    if g == 0 and quar == 0:
                        v.wait_ge(DVc.h, dv_s5)
                    v.tensor_scalar(o_int, o_in[b], sclbc[:, 9:10], None, ALU.mult).then_inc(DVc.h, 1)
                    DVc.n += 1
                block.vector(o_dv)
                mk = DVc.n

                def o_ac(a, quar=quar):
                    a.wait_ge(DVc.h, mk)
                    a.activation(OB[:, quar * 6 * TW:(quar + 1) * 6 * TW], o_int, AF.Copy,
                                 scale=sclbc[:, 8:9]).then_inc(ACc.h, 1)
                    ACc.n += 1
                    ac_ob_marks.append(ACc.n)
                    ac_ob_all.append(ACc.n)
                block.scalar(o_ac)

            ldp = [0]
            dvo_prev = DVc.n

            def p_sync(sync, t0=t0, ldp=ldp, dvo_prev=dvo_prev):
                sync.wait_ge(DVc.h, dvo_prev)
                sync.dma_start(out=p_part.rearrange("p (k t) -> p k t", k=K8),
                               in_=s_part[:, :, t0:t0 + TW].rearrange("k p t -> p k t")).then_inc(LDG.h, 16)
                LDG.n += 16
                sync.dma_start(out=p_xt.rearrange("p (k t) -> p k t", k=K8),
                               in_=xfm.rearrange("(k p) t -> p k t", p=128)[:, :, t0:t0 + TW]).then_inc(LDG.h, 16)
                LDG.n += 16
                ldp[0] = LDG.n
            block.gpsimd(p_sync)

            dv_dn_marks = []
            for h in range(K8):
                b = h % 2
                idx = g * K8 + h
                ldw = [0]

                def w_sync(sync, h=h, b=b, ldw=ldw, idx=idx):
                    if idx >= 2:
                        sync.wait_ge(PEc.h, pe_dn_marks[idx - 2])
                    sync.dma_start(out=wdnb[b].rearrange("p (i c) -> p i c", c=128),
                                   in_=s_wdnT[h, :, :, :].rearrange("i p c -> p i c")).then_inc(LDG.h, 16)
                    LDG.n += 16
                    ldw[0] = LDG.n
                block.gpsimd(w_sync)

                def d_pe(t, h=h, b=b, ldw=ldw, idx=idx):
                    t.wait_ge(LDG.h, ldw[0])
                    t.wait_ge(ACc.h, ac_ob_marks[3])
                    if idx >= 2:
                        t.wait_ge(ACc.h, ac_dn_marks[idx - 2])
                    pp = psRot[h % 2]
                    for i in range(IC):
                        mm = t.matmul(pp[:, :], wdnb[b][:, i * 128:(i + 1) * 128],
                                      OB[:, i * TW:(i + 1) * TW], start=(i == 0), stop=(i == IC - 1))
                    mm.then_inc(PEc.h, 1)
                    PEc.n += 1
                    pe_dn_marks.append(PEc.n)
                block.tensor(d_pe)
                pem = PEc.n

                def d_ac(a, h=h, pem=pem):
                    a.wait_ge(PEc.h, pem)
                    if h >= 2:
                        a.wait_ge(DVc.h, dv_dn_marks[h - 2])
                    a.activation(p_ffl[h % 2], psRot[h % 2][:, :], AF.Copy).then_inc(ACc.h, 1)
                    ACc.n += 1
                    ac_dn_marks.append(ACc.n)
                block.scalar(d_ac)
                acm = ACc.n

                def d_dv(v, h=h, ldp=ldp, acm=acm):
                    v.wait_ge(ACc.h, acm)
                    v.wait_ge(LDG.h, ldp[0])
                    v.tensor_tensor(p_part[:, h * TW:(h + 1) * TW], p_part[:, h * TW:(h + 1) * TW],
                                    p_ffl[h % 2], ALU.add)
                    v.tensor_tensor(p_part[:, h * TW:(h + 1) * TW], p_part[:, h * TW:(h + 1) * TW],
                                    p_xt[:, h * TW:(h + 1) * TW], ALU.add).then_inc(DVc.h, 1)
                    DVc.n += 1
                    dv_dn_marks.append(DVc.n)
                block.vector(d_dv)
            last_tile_pe[0] = PEc.n
            dv_out = DVc.n

            def out_sync(sync, t0=t0, dv_out=dv_out):
                sync.wait_ge(DVc.h, dv_out)
                sync.dma_start(out=outfm.rearrange("(k p) t -> p k t", p=128)[:, :, t0:t0 + TW],
                               in_=p_part.rearrange("p (k t) -> p k t", k=K8)).then_inc(SO.h, 16)
                SO.n += 16
            block.sync(out_sync)

        fin = SO.n

        def fin_sync(sync):
            sync.wait_ge(SO.h, fin)
        block.sync(fin_sync)

    st_.close()
    return nc


def make_inputs(core, TOK, x_full_fm_shard, w):
    import ml_dtypes
    b_qkv = w["b_qkv"]
    im = {
        "xfm": np.ascontiguousarray(x_full_fm_shard),
        "wqkv": np.ascontiguousarray(w["w_qkv"]),
        "wgate_a": np.ascontiguousarray(w["w_gate_a"]),
        "wproj": np.ascontiguousarray(w["w_proj"]),
        "bqk": np.ascontiguousarray(np.concatenate([
            b_qkv[0:1024].reshape(8, 128).T,
            b_qkv[1024:2048].reshape(8, 128).T,
            w["b_gate"].reshape(8, 128).T], axis=1).astype(np.float32)),
        "bv_row": np.ascontiguousarray(b_qkv[2048:3072].reshape(1, 1024)),
        "bproj_d": np.ascontiguousarray(w["b_proj"].reshape(8, 128).T),
        "lng_d": np.ascontiguousarray(w["ln_g"].reshape(8, 128).T),
        "lnb_d": np.ascontiguousarray(w["ln_b"].reshape(8, 128).T),
        "wup_d": np.ascontiguousarray(w["w_up"]),
        "wgf_d": np.ascontiguousarray(w["w_gate_f"]),
        "wdn_d": np.ascontiguousarray(w["w_down"]),
        "identb_d": np.eye(128, dtype=np.float32).astype(ml_dtypes.bfloat16),
        "identf_d": np.eye(128, dtype=np.float32),
        "onesrow_d": np.ones((1, 128), dtype=np.float32),
        "onescol_d": np.ones((128, 1), dtype=np.float32).astype(ml_dtypes.bfloat16),
        "mask_d": np.ascontiguousarray(
            np.concatenate([np.triu(np.ones((128, 128), dtype=np.float32)),
                            np.ones((128, TW - 128), dtype=np.float32)], axis=1)),
        "flagr_d": np.full((128, 1), 1.0 if core % 2 == 1 else 0.0, dtype=np.float32),
        "flags_d": np.full((128, 1), 0.0 if core % 2 == 1 else 1.0, dtype=np.float32),
    }
    return im


def kernel_run(x, w, TOK, dbg=False):
    from concourse.bass_utils import run_bass_kernel_spmd
    B, T, Dm = x.shape
    n = 8
    flat = np.ascontiguousarray(x.reshape(B * T, Dm))
    assert B * T == n * TOK
    nc = build(TOK, dbg=dbg)
    in_maps = []
    for c in range(n):
        shard = np.ascontiguousarray(flat[c * TOK:(c + 1) * TOK, :].T)
        in_maps.append(make_inputs(c, TOK, shard, w))
    res = run_bass_kernel_spmd(nc, in_maps, core_ids=list(range(n)))
    outs = [np.ascontiguousarray(r["outfm"].T) for r in res.results]
    full = np.concatenate(outs, axis=0).reshape(B, T, Dm)
    return full, res

def _stage_weights(L0_attn_w_qkv, L0_attn_b_qkv, L0_attn_w_gate, L0_attn_b_gate,
                   L0_attn_w_proj, L0_attn_b_proj, L0_attn_ln_g, L0_attn_ln_b,
                   L0_ffn_w_gate, L0_ffn_w_up, L0_ffn_w_down):
    return {
        "w_qkv": np.ascontiguousarray(L0_attn_w_qkv, dtype=np.float32),
        "b_qkv": np.ascontiguousarray(L0_attn_b_qkv, dtype=np.float32),
        "w_gate_a": np.ascontiguousarray(L0_attn_w_gate, dtype=np.float32),
        "b_gate": np.ascontiguousarray(L0_attn_b_gate, dtype=np.float32),
        "w_proj": np.ascontiguousarray(L0_attn_w_proj, dtype=np.float32),
        "b_proj": np.ascontiguousarray(L0_attn_b_proj, dtype=np.float32),
        "ln_g": np.ascontiguousarray(L0_attn_ln_g, dtype=np.float32),
        "ln_b": np.ascontiguousarray(L0_attn_ln_b, dtype=np.float32),
        "w_up": np.ascontiguousarray(L0_ffn_w_up, dtype=np.float32),
        "w_gate_f": np.ascontiguousarray(L0_ffn_w_gate, dtype=np.float32),
        "w_down": np.ascontiguousarray(L0_ffn_w_down, dtype=np.float32),
    }


def kernel(L0_attn_x, L0_attn_w_qkv, L0_attn_b_qkv, L0_attn_w_gate, L0_attn_b_gate,
           L0_attn_w_proj, L0_attn_b_proj, L0_attn_ln_g, L0_attn_ln_b,
           L0_ffn_w_gate, L0_ffn_w_up, L0_ffn_w_down):
    from concourse.bass_utils import run_bass_kernel_spmd
    x = np.ascontiguousarray(L0_attn_x, dtype=np.float32)
    B, T, Dm = x.shape
    n = 8
    TOKf = (B * T) // n
    w = _stage_weights(L0_attn_w_qkv, L0_attn_b_qkv, L0_attn_w_gate, L0_attn_b_gate,
                       L0_attn_w_proj, L0_attn_b_proj, L0_attn_ln_g, L0_attn_ln_b,
                       L0_ffn_w_gate, L0_ffn_w_up, L0_ffn_w_down)
    nc = build(TOKf)
    flat = x.reshape(B * T, Dm)
    in_maps = []
    for c in range(n):
        shard = np.ascontiguousarray(flat[c * TOKf:(c + 1) * TOKf, :].T)
        in_maps.append(make_inputs(c, TOKf, shard, w))
    res = run_bass_kernel_spmd(nc, in_maps, core_ids=list(range(n)))
    outs = [np.ascontiguousarray(r["outfm"].T) for r in res.results]
    return np.concatenate(outs, axis=0).reshape(B, T, Dm)
